# version 23
# baseline (speedup 1.0000x reference)
import numpy as np
from contextlib import ExitStack

import concourse.bass as bass
import concourse.mybir as mybir
from concourse.bass_utils import run_bass_kernel_spmd

F32 = mybir.dt.float32
F32R = mybir.dt.float32r
BF16 = mybir.dt.bfloat16
I8 = mybir.dt.int8
AF = mybir.ActivationFunctionType
ALU = mybir.AluOpType
AXX = mybir.AxisListType.X

T = 4096
D = 1024
C = 128
NCH = T // C
TOWN = 2048
NBLK = TOWN // C
INNER = 4096
NGRP = 4
GTOK = 512


class Sems:
    def __init__(self, es, nc, names):
        self.h = {n: es.enter_context(nc.semaphore(n)) for n in names}
        self.c = {n: 0 for n in names}

    def inc(self, instr, name, n=1):
        instr.then_inc(self.h[name], n)
        self.c[name] += n
        return self.c[name]

    def val(self, name):
        return self.c[name]

    def wait(self, eng, name, val):
        if val > 0:
            eng.wait_ge(self.h[name], val)

    def guard(self, eng, instr, name):
        self.inc(instr, name)
        eng.wait_ge(self.h[name], self.c[name])


def build_nc(nch=NCH, do_ffn=True, do_ag=True, dbg=(), mix_stop=0):
    nc = bass.Bass("TRN2", target_bir_lowering=False, debug=False,
                   detect_race_conditions=False)

    ap_x = nc.dram_tensor("x", [T, D], F32, kind="ExternalInput").ap()
    ap_xr = nc.dram_tensor("xr", [TOWN, D], F32, kind="ExternalInput").ap()
    ap_wqT = nc.dram_tensor("wqT", [D, D], F32, kind="ExternalInput").ap()
    ap_wkT = nc.dram_tensor("wkT", [D, D], F32, kind="ExternalInput").ap()
    ap_wvT = nc.dram_tensor("wvT", [D, D], F32, kind="ExternalInput").ap()
    ap_wfT = nc.dram_tensor("wfT", [D, D], F32, kind="ExternalInput").ap()
    ap_woT = nc.dram_tensor("woT", [D, D], F32, kind="ExternalInput").ap()
    ap_gw = nc.dram_tensor("gw", [128, 8], F32, kind="ExternalInput").ap()
    ap_wgT = nc.dram_tensor("wgT", [D, INNER], F32, kind="ExternalInput").ap()
    ap_wuT = nc.dram_tensor("wuT", [D, INNER], F32, kind="ExternalInput").ap()
    ap_wdT = nc.dram_tensor("wdT", [INNER, D], F32, kind="ExternalInput").ap()
    ap_hp = nc.dram_tensor("hpc", [128, 1], F32, kind="ExternalInput").ap()
    ap_out = nc.dram_tensor("out", [TOWN, D], F32, kind="ExternalOutput").ap()

    d_wgq = nc.dram_tensor("wgq_s", [D, INNER], BF16).ap()
    d_wuq = nc.dram_tensor("wuq_s", [D, INNER], BF16).ap()
    d_wdq = nc.dram_tensor("wdq_s", [INNER, D], BF16).ap()
    d_ccout = nc.dram_tensor("cc_out", [1024, T], BF16)

    dbg_aps = {}
    for name, shape in dict(dbg).items():
        dbg_aps[name] = nc.dram_tensor(name, shape, F32, kind="ExternalOutput").ap()

    es = ExitStack()
    sb = lambda n, s, dt=F32: es.enter_context(nc.sbuf_tensor(n, s, dt))
    pst = lambda n, s: es.enter_context(nc.psum_tensor(n, s, F32))

    with es:
        S = Sems(es, nc, ["qx0", "qx1", "qt0", "qt1", "qg0", "qg1", "qd0", "qd1",
                          "qw", "qst", "qcc", "qcb", "qv",
                          "qa0", "qa1", "qs0", "qs1",
                          "pe", "dv", "sc", "gp", "cs", "mk"])

        identb = sb("identb", [128, 128], BF16)
        identf = sb("identf", [128, 128], F32)
        utones = sb("utones", [128, 128], F32R)
        utonesb = sb("utonesb", [128, 128], BF16)
        ones_row = sb("ones_row", [1, 128], F32R)
        ones_col = sb("ones_col", [128, 1], F32R)
        one_one = sb("one_one", [1, 1], F32R)
        hpcol = sb("hpcol", [128, 1], F32)
        gwcol = sb("gwcol", [128, 8], F32)
        woTg = sb("woTg", [128, 8 * 1024], BF16)
        inv_swg = sb("inv_swg", [128, 1], F32)
        inv_swu = sb("inv_swu", [128, 1], F32)
        inv_swd = sb("inv_swd", [128, 1], F32)

        block = es.enter_context(nc.Block())

        esB = ExitStack()
        sbB = lambda n, s, dt=F32: esB.enter_context(nc.sbuf_tensor(n, s, dt))
        pstB = lambda n, s: esB.enter_context(nc.psum_tensor(n, s, F32))

        esC = ExitStack()
        cscr = esC.enter_context(nc.sbuf_tensor("cscr", [128, 128], F32))

        @block.gpsimd
        def _(g):
            g.memset(identb[:], 0.0)
            g.memset(identf[:], 0.0)
            i = g.memset(cscr[:], 0.0)
            S.inc(i, "mk")
            S.wait(g, "mk", S.val("mk"))
            g.affine_select(out=identb[:], in_=identb[:], compare_op=ALU.not_equal,
                            fill=1.0, base=0, pattern=[[-1, 128]], channel_multiplier=1)
            g.affine_select(out=identf[:], in_=identf[:], compare_op=ALU.not_equal,
                            fill=1.0, base=0, pattern=[[-1, 128]], channel_multiplier=1)
            i = g.affine_select(out=cscr[:], in_=cscr[:], compare_op=ALU.is_gt,
                                fill=1.0, base=0, pattern=[[-1, 128]], channel_multiplier=1)
            S.inc(i, "gp")
        mk_masks0 = S.val("gp")

        @block.vector
        def _(v, mk_masks0=mk_masks0):
            S.wait(v, "gp", mk_masks0)
            v.tensor_copy(out=utones[:], in_=cscr[:])
            i = v.tensor_copy(out=utonesb[:], in_=cscr[:])
            S.guard(v, i, "dv")
            i = v.memset(cscr[:, 0:1], 1.0)
            S.guard(v, i, "dv")
            i = v.memset(cscr[0:1, 0:128], 1.0)
            S.guard(v, i, "dv")
            v.tensor_copy(out=ones_col[:], in_=cscr[:, 0:1])
            v.tensor_copy(out=ones_row[:], in_=cscr[0:1, 0:128])
            i = v.tensor_copy(out=one_one[:], in_=cscr[0:1, 0:1])
            S.inc(i, "dv")
        mk_consts = S.val("dv")

        @block.gpsimd
        def _(g, mk_consts=mk_consts):
            S.wait(g, "dv", mk_consts)
            i = g.engine_nop()
            S.inc(i, "gp")
        mk_masks = S.val("gp")
        esC.close()

        wq_b = sbB("wq_b", [128, 8 * 512], BF16)
        wk_b = sbB("wk_b", [128, 8 * 512], BF16)
        wv_b = sbB("wv_b", [128, 8 * 512], BF16)
        wf_b = sbB("wf_b", [128, 8 * 512], BF16)
        woT_f = sbB("woT_f", [128, 8 * 1024], F32)

        @block.gpsimd
        def _(g):
            g.dma_start(out=hpcol[:], in_=ap_hp[:, :]).then_inc(S.h["qw"], 16)
            g.wait_ge(S.h["qw"], S.c["qw"] + 16)
            g.dma_start(out=gwcol[:], in_=ap_gw[:, :]).then_inc(S.h["qw"], 16)
            g.wait_ge(S.h["qw"], S.c["qw"] + 32)
            g.dma_start(out=woT_f[:].rearrange("p (b f) -> p b f", b=8),
                        in_=ap_woT.rearrange("(b p) f -> p b f", p=128)
                        ).then_inc(S.h["qw"], 16)
        S.c["qw"] += 3 * 16
        mk_wload = S.val("qw")

        @block.vector
        def _(v):
            S.wait(v, "qw", mk_wload)
            for cc in range(8):
                i = v.tensor_scalar_mul(out=woTg[:, cc * 1024:(cc + 1) * 1024],
                                        in0=woT_f[:, cc * 1024:(cc + 1) * 1024],
                                        scalar1=gwcol[:, cc:cc + 1])
            S.inc(i, "dv")
        mk_woTg = S.val("dv")

        if do_ffn:
            esW = ExitStack()
            sbW = lambda n, s, dt=F32: esW.enter_context(nc.sbuf_tensor(n, s, dt))
            wtmp0 = sbW("wtmp0", [128, 4096], F32)
            wtmp1 = sbW("wtmp1", [128, 4096], F32)
            wi8 = sbW("wi8", [128, 4096], I8)
            wqb0 = sbW("wqb0", [128, 4096], BF16)
            wqb1 = sbW("wqb1", [128, 4096], BF16)
            mpart = sbW("mpart", [128, 32], F32)
            mcol1 = sbW("mcol1", [128, 1], F32)
            sc11 = sbW("sc11", [1, 4], F32)
            swcol = sbW("swcol", [128, 1], F32)
            ps_w = esW.enter_context(nc.psum_tensor("ps_w", [128, 512], F32))

            q_marks = [0, 0]
            st_tail = {0: 0, 1: 0}
            for apw, dscr, ntile, tw, invcol in (
                    (ap_wgT, d_wgq, 8, 4096, inv_swg),
                    (ap_wuT, d_wuq, 8, 4096, inv_swu),
                    (ap_wdT, d_wdq, 32, 1024, inv_swd)):
                red_marks = [max(q_marks), max(q_marks)]
                for it in range(ntile):
                    p = it % 2
                    buf = wtmp0 if p == 0 else wtmp1

                    @block.sync
                    def _(sy, it=it, apw=apw, tw=tw, buf=buf, p=p,
                          mk_free=red_marks[it]):
                        S.wait(sy, "dv", mk_free)
                        S.wait(sy, f"qa{p}", S.val(f"qa{p}"))
                        sy.dma_start(out=buf[:, :tw],
                                     in_=apw[it * 128:(it + 1) * 128, :]
                                     ).then_inc(S.h[f"qa{p}"], 16)
                    S.c[f"qa{p}"] += 16
                    mk = S.val(f"qa{p}")

                    @block.vector
                    def _(v, it=it, tw=tw, mk=mk, buf=buf, p=p):
                        S.wait(v, f"qa{p}", mk)
                        i = v.reduce_sum(out=mpart[:, it:it + 1], in_=buf[:, :tw],
                                         axis=AXX, apply_absolute_value=True)
                        S.inc(i, "dv")
                    red_marks.append(S.val("dv"))

                @block.vector
                def _(v, ntile=ntile, m=red_marks[-1]):
                    S.wait(v, "dv", m)
                    with nc.allow_low_precision(reason="mean|w| scale, 1e-4 rel is ample"):
                        i = v.reduce_sum(out=mcol1[:], in_=mpart[:, :ntile], axis=AXX)
                    S.inc(i, "dv")
                mkc = S.val("dv")

                @block.tensor
                def _(t, mkc=mkc):
                    S.wait(t, "dv", max(mkc, mk_consts))
                    i = t.matmul(out=ps_w[0:1, 0:1], lhsT=mcol1[:],
                                 rhs=ones_col[:].bitcast(F32),
                                 start=True, stop=True)
                    S.inc(i, "pe")
                mk_tot = S.val("pe")

                @block.vector
                def _(v, mk_tot=mk_tot):
                    S.wait(v, "pe", mk_tot)
                    with nc.allow_low_precision(reason="weight-quant scale, 1e-4 rel ample"):
                        i = v.tensor_scalar(out=sc11[:, 0:1], in0=ps_w[0:1, 0:1],
                                            scalar1=1.0 / (INNER * D), scalar2=1e-5,
                                            op0=ALU.mult, op1=ALU.max)
                        S.guard(v, i, "dv")
                        i = v.reciprocal(out=sc11[:, 1:2], in_=sc11[:, 0:1])
                    S.inc(i, "dv")
                mk_sc11 = S.val("dv")

                @block.tensor
                def _(t, mk_sc11=mk_sc11):
                    S.wait(t, "dv", mk_sc11)
                    t.matmul(out=ps_w[:, 1:2], lhsT=ones_row[:].bitcast(F32),
                             rhs=sc11[0:1, 0:1], start=True, stop=True)
                    i = t.matmul(out=ps_w[:, 2:3], lhsT=ones_row[:].bitcast(F32),
                                 rhs=sc11[0:1, 1:2], start=True, stop=True)
                    S.inc(i, "pe")
                mk_bc = S.val("pe")

                @block.vector
                def _(v, invcol=invcol, mk_bc=mk_bc):
                    S.wait(v, "pe", mk_bc)
                    v.tensor_copy(out=invcol[:], in_=ps_w[:, 1:2])
                    i = v.tensor_copy(out=swcol[:], in_=ps_w[:, 2:3])
                    S.inc(i, "dv")
                mk_sw = S.val("dv")

                q_marks = [mk_sw, mk_sw]
                st_marks_w = [st_tail[0], st_tail[1]]
                for it in range(ntile):
                    p = it % 2
                    buf = wtmp0 if p == 0 else wtmp1
                    qb = wqb0 if p == 0 else wqb1

                    @block.sync
                    def _(sy, it=it, apw=apw, tw=tw, buf=buf, p=p,
                          mk_free=q_marks[it]):
                        S.wait(sy, "dv", mk_free)
                        S.wait(sy, f"qa{p}", S.val(f"qa{p}"))
                        sy.dma_start(out=buf[:, :tw],
                                     in_=apw[it * 128:(it + 1) * 128, :]
                                     ).then_inc(S.h[f"qa{p}"], 16)
                    S.c[f"qa{p}"] += 16
                    mk = S.val(f"qa{p}")

                    @block.vector
                    def _(v, tw=tw, mk=mk, mk_sw=mk_sw, buf=buf, qb=qb, p=p,
                          mk_st=st_marks_w[it], mk_pq=q_marks[-1]):
                        S.wait(v, f"qa{p}", mk)
                        S.wait(v, "dv", max(mk_sw, mk_pq))
                        S.wait(v, f"qs{p}", mk_st)
                        i = v.tensor_scalar_mul(out=wi8[:, :tw], in0=buf[:, :tw],
                                                scalar1=swcol[:])
                        S.guard(v, i, "dv")
                        i = v.tensor_scalar(out=qb[:, :tw], in0=wi8[:, :tw],
                                            scalar1=1.0, scalar2=-1.0,
                                            op0=ALU.min, op1=ALU.max)
                        S.inc(i, "dv")
                    q_marks.append(S.val("dv"))
                    mkq = S.val("dv")

                    @block.gpsimd
                    def _(g, it=it, dscr=dscr, tw=tw, mkq=mkq, qb=qb, p=p):
                        S.wait(g, "dv", mkq)
                        S.wait(g, f"qs{p}", S.val(f"qs{p}"))
                        g.dma_start(out=dscr[it * 128:(it + 1) * 128, :],
                                    in_=qb[:, :tw]).then_inc(S.h[f"qs{p}"], 16)
                    S.c[f"qs{p}"] += 16
                    st_marks_w.append(S.val(f"qs{p}"))
                q_marks = q_marks[-2:]
                st_tail = {0: S.val("qs0"), 1: S.val("qs1")}
            mk_wq0 = S.val("qs0")
            mk_wq1 = S.val("qs1")

            esW.close()
        else:
            mk_wq0 = mk_wq1 = 0

        xin = sbB("xin", [128, 2 * D], F32)
        xnbf = sbB("xnbf", [128, 2 * D], BF16)
        xnT = sbB("xnT", [128, 2 * 1024], BF16)
        lncol = sbB("lncol", [128, 8], F32)
        lnst = sbB("lnst", [128, 12], F32)
        QTb = sbB("QTb", [128, 2 * 512], BF16)
        Kb = sbB("Kb", [128, 512], BF16)
        KTb = sbB("KTb", [128, 2 * 512], BF16)
        Vb = sbB("Vb", [128, 2 * 512], BF16)
        spf = sbB("spf", [128, 512], F32R)
        etmp = sbB("etmp", [128, 512], F32)
        stb = sbB("stb", [128, 2 * 512], BF16)
        Df = sbB("Df", [128, 2 * 512], F32)
        Dinv = sbB("Dinv", [128, 512], F32)
        dcrow = sbB("dcrow", [1, 512], F32R)
        dccol = sbB("dccol", [128, 4], F32)
        atm = sbB("atm", [128, 128], BF16)
        btm = sbB("btm", [128, 128], BF16)
        stT = sbB("stT", [128, 256], BF16)
        qtT = sbB("qtT", [128, 256], BF16)
        okf = sbB("okf", [128, 256], F32)
        eok = sbB("eok", [128, 256], F32)
        qtb = sbB("qtb", [128, 256], BF16)
        smcol = sbB("smcol", [128, 4], F32)
        s1f = sbB("s1f", [128, 1024], F32)
        s2f = sbB("s2f", [128, 1024], F32)
        s1b = sbB("s1b", [128, 3 * 1024], BF16)
        s2b = sbB("s2b", [128, 3 * 1024], BF16)
        tmpu = sbB("tmpu", [128, 1024], F32)
        sf = tmpu[:, 512:1024]
        ov_sb = sbB("ov_sb", [128, 4 * T], BF16)

        ps_q = pstB("ps_q", [128, 512])
        ps_k = pstB("ps_k", [128, 512])
        ps_v = pstB("ps_v", [128, 512])
        ps_f = pstB("ps_f", [128, 512])
        ps_g = pstB("ps_g", [128, 512])
        ps_m1 = pstB("ps_m1", [128, 512])
        ps_m2 = pstB("ps_m2", [128, 512])
        ps_i = pstB("ps_i", [128, 512])

        mk_xn_hist = {}
        mk_xnT_hist = {}
        mk_proj_hist = {}
        slot_xn = {0: 0, 1: 0}
        slot_xnT = {0: 0, 1: 0}
        slot_proj = {0: 0, 1: 0}
        dump_marks = []
        mk_state = 0
        for hpg in range(1 if mix_stop else 2):
            @block.gpsimd
            def _(g, hpg=hpg):
                S.wait(g, "pe", S.val("pe"))
                base = S.val("qw")
                for wi, (apw, dst) in enumerate(((ap_wqT, wq_b), (ap_wkT, wk_b),
                                                 (ap_wvT, wv_b), (ap_wfT, wf_b))):
                    g.wait_ge(S.h["qw"], base + wi * 16)
                    g.dma_start(out=dst[:].rearrange("p (b f) -> p b f", b=8),
                                in_=apw.rearrange("(b p) f -> p b f", p=128)
                                [:, :, hpg * 512:(hpg + 1) * 512]
                                ).then_inc(S.h["qw"], 16)
            S.c["qw"] += 64
            mk_wload_h = S.val("qw")

            @block.vector
            def _(v):
                S.wait(v, "pe", S.val("pe"))
                S.wait(v, "sc", S.val("sc"))
                if hpg == 0 and do_ffn:
                    S.wait(v, "qs0", mk_wq0)
                    S.wait(v, "qs1", mk_wq1)
                if dump_marks:
                    S.wait(v, "qst", dump_marks[-1])
                v.memset(s1f[:], 0.0)
                v.memset(s2f[:], 0.0)
                v.memset(s1b[:], 0.0)
                v.memset(ov_sb[:], 0.0)
                i = v.memset(s2b[:], 0.0)
                S.inc(i, "dv")
            mk_state0h = S.val("dv")

            M = {}

            def emit_LN(c):
                pp = c % 2
                gi = hpg * nch + c
                xslice = xin[:, pp * D:(pp + 1) * D]
                xnbs = xnbf[:, pp * D:(pp + 1) * D]

                prev_xn = slot_xn[pp]

                @block.gpsimd
                def _(g, c=c, pp=pp, prev_xn=prev_xn, gi=gi):
                    S.wait(g, "dv", prev_xn)
                    if gi == 0 and do_ffn:
                        S.wait(g, "qs0", mk_wq0)
                        S.wait(g, "qs1", mk_wq1)
                    S.wait(g, f"qx{pp}", S.val(f"qx{pp}"))
                    g.dma_start(out=xin[:, pp * D:(pp + 1) * D],
                                in_=ap_x[c * C:(c + 1) * C, :]
                                ).then_inc(S.h[f"qx{pp}"], 16)
                S.c[f"qx{pp}"] += 16
                mk_x = S.val(f"qx{pp}")

                @block.vector
                def _(v, mk_x=mk_x, xslice=xslice, pp=pp):
                    S.wait(v, f"qx{pp}", mk_x)
                    v.bn_stats(out=lnst[:, 0:6], in_=xslice[:, 0:512])
                    i = v.bn_stats(out=lnst[:, 6:12], in_=xslice[:, 512:1024])
                    S.guard(v, i, "dv")
                    i = v.bn_aggr(out=lncol[:, 0:2], in_=lnst[:, 0:12])
                    S.guard(v, i, "dv")
                    i = v.tensor_scalar_add(out=lncol[:, 2:3], in0=lncol[:, 1:2],
                                            scalar1=1e-5)
                    S.guard(v, i, "dv")
                    i = v.reciprocal(out=lncol[:, 6:7], in_=lncol[:, 2:3])
                    S.inc(i, "dv")
                mk_ln1 = S.val("dv")

                @block.scalar
                def _(sc, mk_ln1=mk_ln1):
                    S.wait(sc, "dv", mk_ln1)
                    i = sc.activation(out=lncol[:, 3:4], in_=lncol[:, 6:7], func=AF.Sqrt)
                    S.inc(i, "sc")
                mk_rv = S.val("sc")
                M.setdefault(c, {})["rv"] = mk_rv

                @block.vector
                def _(v, mk_rv=mk_rv, xslice=xslice, xnbs=xnbs, gi=gi, pp=pp):
                    S.wait(v, "sc", mk_rv)
                    S.wait(v, f"qt{pp}", slot_xnT[pp])
                    i = v.tensor_scalar_mul(out=lncol[:, 4:5], in0=lncol[:, 0:1],
                                            scalar1=-1.0)
                    S.guard(v, i, "dv")
                    i = v.tensor_scalar(out=xnbs, in0=xslice, scalar1=lncol[:, 4:5],
                                        scalar2=lncol[:, 3:4], op0=ALU.add, op1=ALU.mult)
                    S.inc(i, "dv")
                mk_xn_hist[gi] = S.val("dv")
                slot_xn[pp] = mk_xn_hist[gi]

                @block.sync
                def _(sy, pp=pp, gi=gi, xnbs=xnbs, prev_proj=slot_proj[pp]):
                    S.wait(sy, "dv", mk_xn_hist[gi])
                    S.wait(sy, "pe", prev_proj)
                    S.wait(sy, f"qt{pp}", S.val(f"qt{pp}"))
                    sy.dma_start(out=xnT[:, pp * 1024:(pp + 1) * 1024]
                                 .rearrange("p (b f) -> p b f", b=8),
                                 in_=xnbs.rearrange("p (b f) -> p b f", b=8),
                                 transpose=True).then_inc(S.h[f"qt{pp}"], 16)
                S.c[f"qt{pp}"] += 16
                mk_xnT_hist[gi] = S.val(f"qt{pp}")
                slot_xnT[pp] = mk_xnT_hist[gi]

            def emit_A2a(c):
                pp = c % 2
                gi = hpg * nch + c
                sl2 = pp * 512
                xnTs = lambda dc: xnT[:, pp * 1024 + dc * 128:pp * 1024 + (dc + 1) * 128]

                @block.tensor
                def _(t, pp=pp, c=c, gi=gi,
                      mk_prev=M.get(c - 1, {}).get("state", mk_state0h)):
                    S.wait(t, f"qt{pp}", mk_xnT_hist[gi])
                    S.wait(t, "dv", mk_prev)
                    if c == 0:
                        S.wait(t, "qw", mk_wload_h)
                        S.wait(t, "gp", mk_masks)
                        if dump_marks:
                            S.wait(t, "qst", dump_marks[-1])
                    for dst_ps, wsb in ((ps_f, wf_b), (ps_k, wk_b)):
                        for dc in range(8):
                            i = t.matmul(out=dst_ps[:], lhsT=xnTs(dc),
                                         rhs=wsb[:, dc * 512:(dc + 1) * 512],
                                         start=(dc == 0), stop=(dc == 7))
                        S.inc(i, "pe")
                mk_fk = S.val("pe")

                @block.scalar
                def _(sc, mk_fk=mk_fk):
                    S.wait(sc, "pe", mk_fk - 1)
                    i = sc.activation(out=etmp[:], in_=ps_f[:], func=AF.Exp, scale=-1.0)
                    S.guard(sc, i, "sc")
                    i = sc.activation(out=spf[:], in_=etmp[:], func=AF.Ln, bias=1.0)
                    S.inc(i, "sc")
                mk_spf = S.val("sc")

                @block.tensor
                def _(t, mk_spf=mk_spf, mk_pst=M.get(c - 1, {}).get("st", 0)):
                    S.wait(t, "sc", mk_spf)
                    S.wait(t, "dv", mk_pst)
                    t.matmul(out=ps_g[:], lhsT=utones[:], rhs=spf[:],
                             start=True, stop=True)
                    i = t.matmul(out=ps_m2[0:1, 0:512], lhsT=ones_col[:], rhs=spf[:],
                                 start=True, stop=True)
                    S.inc(i, "pe")
                mk_ktg = S.val("pe")

                @block.scalar
                def _(sc, mk_ktg=mk_ktg, sl2=sl2):
                    S.wait(sc, "pe", mk_ktg)
                    sc.activation(out=Df[:, sl2:sl2 + 512], in_=ps_g[:], func=AF.Exp,
                                  scale=-0.125)
                    sc.activation(out=Dinv[:], in_=ps_g[:], func=AF.Exp, scale=0.125)
                    i = sc.activation(out=dcrow[:], in_=ps_m2[0:1, 0:512], func=AF.Exp,
                                      scale=-0.125)
                    S.guard(sc, i, "sc")
                    i = sc.activation(out=etmp[:], in_=spf[:], func=AF.Exp, scale=-0.125)
                    S.inc(i, "sc")
                M[c]["dd"] = S.val("sc")
                mk_dd = M[c]["dd"]

                @block.tensor
                def _(t, pp=pp, xnTs=xnTs):
                    for j in range(4):
                        for dc in range(8):
                            i = t.matmul(out=ps_q[:, j * 128:(j + 1) * 128],
                                         lhsT=wq_b[:, dc * 512 + j * 128:dc * 512 + (j + 1) * 128],
                                         rhs=xnTs(dc), start=(dc == 0), stop=(dc == 7))
                    S.inc(i, "pe")
                    for dc in range(8):
                        i = t.matmul(out=ps_v[:], lhsT=xnTs(dc),
                                     rhs=wv_b[:, dc * 512:(dc + 1) * 512],
                                     start=(dc == 0), stop=(dc == 7))
                    S.inc(i, "pe")
                mk_proj_hist[gi] = S.val("pe")
                slot_proj[pp] = mk_proj_hist[gi]
                mk_proj = mk_proj_hist[gi]

                @block.tensor
                def _(t, mk_dd=mk_dd):
                    S.wait(t, "sc", mk_dd)
                    for ms in range(4):
                        i = t.matmul(out=ps_m2[:, ms * 128:(ms + 1) * 128],
                                     lhsT=dcrow[0:1, ms * 128:(ms + 1) * 128],
                                     rhs=ones_row[:], start=True, stop=True)
                    S.inc(i, "pe")
                    i = t.matmul(out=ps_g[:], lhsT=ones_row[:], rhs=dcrow[:],
                                 start=True, stop=True)
                    S.inc(i, "pe")
                mk_dc = S.val("pe")

                @block.scalar
                def _(sc, mk_proj=mk_proj, sl2=sl2):
                    S.wait(sc, "pe", mk_proj)
                    sc.activation(out=QTb[:, sl2:sl2 + 512], in_=ps_q[:], func=AF.Silu)
                    sc.activation(out=Kb[:], in_=ps_k[:], func=AF.Silu)
                    i = sc.activation(out=Vb[:, sl2:sl2 + 512], in_=ps_v[:], func=AF.Silu)
                    S.inc(i, "sc")
                M[c]["silu"] = S.val("sc")
                mk_silu = M[c]["silu"]

                @block.vector
                def _(v, mk_dc=mk_dc, mk_dd=mk_dd, sl2=sl2,
                      mk_pstate=M.get(c - 1, {}).get("state", 0)):
                    S.wait(v, "sc", mk_dd)
                    S.wait(v, "pe", mk_dc)
                    S.wait(v, "dv", mk_pstate)
                    v.tensor_copy(out=dccol[:], in_=ps_m2[:, 0:512:128])
                    i = v.tensor_scalar(out=sf, in0=etmp[:], scalar1=-1.0, scalar2=1.0,
                                        op0=ALU.mult, op1=ALU.add)
                    S.guard(v, i, "dv")
                    i = v.tensor_tensor(out=stb[:, sl2:sl2 + 512], in0=sf, in1=Dinv[:],
                                        op=ALU.mult)
                    S.inc(i, "dv")
                M[c]["st"] = S.val("dv")

                @block.tensor
                def _(t, mk_silu=mk_silu,
                      mk_chg=M.get(c - 1, {}).get("atm0", 0)):
                    S.wait(t, "sc", mk_silu)
                    S.wait(t, "dv", mk_chg)
                    for j in range(4):
                        i = t.matmul(out=ps_m1[:, j * 128:(j + 1) * 128],
                                     lhsT=Kb[:, j * 128:(j + 1) * 128], rhs=identb[:],
                                     start=True, stop=True)
                    S.inc(i, "pe")
                mk_ktm = S.val("pe")

                @block.vector
                def _(v, mk_ktm=mk_ktm, sl2=sl2):
                    S.wait(v, "pe", mk_ktm)
                    i = v.tensor_copy(out=KTb[:, sl2:sl2 + 512], in_=ps_m1[:])
                    S.inc(i, "dv")
                M[c]["ktb"] = S.val("dv")

            def emit_B1_mm(c):
                sl2 = (c % 2) * 512
                mk_st = M[c]["st"]

                @block.tensor
                def _(t, mk_st=mk_st, sl2=sl2):
                    S.wait(t, "dv", mk_st)
                    for h, (pu1, pu2) in enumerate(((ps_q, ps_k), (ps_v, ps_f))):
                        hs = sl2 + h * 256
                        for dkh in range(2):
                            t.matmul(out=pu1[:, dkh * 256:(dkh + 1) * 256],
                                     lhsT=Kb[:, h * 256 + dkh * 128:h * 256 + (dkh + 1) * 128],
                                     rhs=stb[:, hs:hs + 256], start=True, stop=True)
                        for mh in range(2):
                            i = t.matmul(out=pu2[:, mh * 256:(mh + 1) * 256],
                                         lhsT=stb[:, hs + mh * 128:hs + (mh + 1) * 128],
                                         rhs=Vb[:, hs:hs + 256], start=True, stop=True)
                        S.inc(i, "pe")
                M[c]["u"] = S.val("pe")

            def emit_B1_vec(c):
                wsl = (c % 3) * 1024
                mk_u = M[c]["u"]

                @block.vector
                def _(v, mk_u=mk_u, wsl=wsl):
                    for h, (pu1, pu2) in enumerate(((ps_q, ps_k), (ps_v, ps_f))):
                        S.wait(v, "pe", mk_u - 1 + h)
                        for dkh in range(2):
                            sl = slice(h * 512 + dkh * 256, h * 512 + (dkh + 1) * 256)
                            v.tensor_tensor(out=tmpu[:, dkh * 256:(dkh + 1) * 256],
                                            in0=pu1[:, dkh * 256:(dkh + 1) * 256],
                                            in1=s1f[:, sl], op=ALU.add)
                        for mh in range(2):
                            sl = slice(h * 512 + mh * 256, h * 512 + (mh + 1) * 256)
                            i = v.tensor_tensor(out=tmpu[:, 512 + mh * 256:768 + mh * 256],
                                                in0=pu2[:, mh * 256:(mh + 1) * 256],
                                                in1=s2f[:, sl], op=ALU.add)
                        S.guard(v, i, "dv")
                        for dkh in range(2):
                            sl = slice(h * 512 + dkh * 256, h * 512 + (dkh + 1) * 256)
                            slw = slice(wsl + h * 512 + dkh * 256,
                                        wsl + h * 512 + (dkh + 1) * 256)
                            ts = tmpu[:, dkh * 256:(dkh + 1) * 256]
                            v.tensor_tensor(out=s1f[:, sl], in0=ts,
                                            in1=ps_g[:, h * 256:(h + 1) * 256],
                                            op=ALU.mult)
                            v.tensor_tensor(out=s1b[:, slw], in0=ts,
                                            in1=ps_g[:, h * 256:(h + 1) * 256],
                                            op=ALU.mult)
                        for mh in range(2):
                            sl = slice(h * 512 + mh * 256, h * 512 + (mh + 1) * 256)
                            slw = slice(wsl + h * 512 + mh * 256,
                                        wsl + h * 512 + (mh + 1) * 256)
                            ts = tmpu[:, 512 + mh * 256:768 + mh * 256]
                            v.tensor_scalar_mul(out=s2f[:, sl], in0=ts,
                                                scalar1=dccol[:, h * 2 + mh:h * 2 + mh + 1])
                            i = v.tensor_scalar_mul(out=s2b[:, slw], in0=ts,
                                                    scalar1=dccol[:, h * 2 + mh:h * 2 + mh + 1])
                        if h == 0:
                            S.guard(v, i, "dv")
                        else:
                            S.inc(i, "dv")
                M[c]["state"] = S.val("dv")

            def chainA(c, h, mk_extra=0):
                sl2 = (c % 2) * 512
                hs = sl2 + h * 256

                @block.tensor
                def _(t, hs=hs, mk_w=max(M[c]["st"], mk_extra)):
                    S.wait(t, "dv", mk_w)
                    for dkh in range(2):
                        i = t.matmul(out=ps_m1[:, 0:128],
                                     lhsT=KTb[:, hs + dkh * 128:hs + (dkh + 1) * 128],
                                     rhs=QTb[:, hs + dkh * 128:hs + (dkh + 1) * 128],
                                     start=(dkh == 0), stop=(dkh == 1))
                    for mh in range(2):
                        i = t.matmul(out=ps_m1[:, 128 + mh * 128:128 + (mh + 1) * 128],
                                     lhsT=stb[:, hs + mh * 128:hs + (mh + 1) * 128],
                                     rhs=identb[:], start=True, stop=True)
                    S.inc(i, "pe")
                mk_at = S.val("pe")

                @block.vector
                def _(v, mk_at=mk_at):
                    S.wait(v, "pe", mk_at)
                    v.tensor_tensor(out=atm[:], in0=ps_m1[:, 0:128], in1=utonesb[:],
                                    op=ALU.mult)
                    i = v.tensor_copy(out=stT[:], in_=ps_m1[:, 128:384])
                    S.inc(i, "dv")
                M[c][f"atm{h}"] = S.val("dv")

            def chainB(c, h):
                sl2 = (c % 2) * 512
                hs = sl2 + h * 256
                rsl = ((c - 1) % 3) * 1024

                @block.tensor
                def _(t, h=h, hs=hs, mk_atm=M[c][f"atm{h}"], rsl=rsl):
                    S.wait(t, "dv", mk_atm)
                    t.matmul(out=ps_i[:, 0:256], lhsT=atm[:], rhs=stb[:, hs:hs + 256],
                             start=True, stop=False)
                    for dkh in range(2):
                        i = t.matmul(out=ps_i[:, 0:256],
                                     lhsT=QTb[:, hs + dkh * 128:hs + (dkh + 1) * 128],
                                     rhs=s1b[:, rsl + h * 512 + dkh * 256:
                                              rsl + h * 512 + (dkh + 1) * 256],
                                     start=False, stop=(dkh == 1))
                    S.inc(i, "pe")
                mk_i1 = S.val("pe")

                @block.vector
                def _(v, hs=hs, mk_i1=mk_i1):
                    S.wait(v, "pe", mk_i1)
                    i = v.tensor_tensor(out=okf[:], in0=ps_i[:, 0:256],
                                        in1=Df[:, hs:hs + 256], op=ALU.mult)
                    S.guard(v, i, "dv")
                    i = v.reduce_max(out=smcol[:, 0:1], in_=okf[:], axis=AXX,
                                     negate=True)
                    S.inc(i, "dv")
                mk_ok = S.val("dv")

                if c == 0 and h == 0 and f"d_g{hpg}_ok" in dbg_aps:
                    @block.gpsimd
                    def _(g, mk_ok=mk_ok, hpg=hpg):
                        S.wait(g, "dv", mk_ok)
                        g.dma_start(out=dbg_aps[f"d_g{hpg}_ok"], in_=okf[:]).then_inc(S.h["qst"], 16)
                        g.dma_start(out=dbg_aps[f"d_g{hpg}_atm"], in_=atm[:]).then_inc(S.h["qst"], 16)
                        g.dma_start(out=dbg_aps[f"d_g{hpg}_s1b"], in_=s1b[:, 0:1024]).then_inc(S.h["qst"], 16)
                    S.c["qst"] += 48

                @block.scalar
                def _(sc, mk_ok=mk_ok, mk_scg=S.val("sc")):
                    S.wait(sc, "dv", mk_ok)
                    S.wait(sc, "sc", mk_scg)
                    i = sc.activation(out=eok[:], in_=okf[:], func=AF.Exp,
                                      bias=smcol[:, 0:1], accum_out=smcol[:, 1:2])
                    S.inc(i, "sc")
                mk_eok = S.val("sc")

                @block.vector
                def _(v, hs=hs, mk_eok=mk_eok):
                    S.wait(v, "sc", mk_eok)
                    i = v.reciprocal(out=smcol[:, 2:3], in_=smcol[:, 1:2])
                    S.guard(v, i, "dv")
                    i = v.scalar_tensor_tensor(out=qtb[:], in0=eok[:],
                                               scalar=smcol[:, 2:3],
                                               in1=Df[:, hs:hs + 256],
                                               op0=ALU.mult, op1=ALU.mult)
                    S.inc(i, "dv")
                M[c][f"qt{h}"] = S.val("dv")

            def chainC(c, h, mk_stcur):
                @block.tensor
                def _(t, mk_w=max(M[c][f"qt{h}"], mk_stcur)):
                    S.wait(t, "dv", mk_w)
                    for mh in range(2):
                        i = t.matmul(out=ps_m1[:, 128 + mh * 128:128 + (mh + 1) * 128],
                                     lhsT=qtb[:, mh * 128:(mh + 1) * 128], rhs=identb[:],
                                     start=True, stop=True)
                    S.inc(i, "pe")
                mk_qtTp = S.val("pe")

                @block.vector
                def _(v, mk_qtTp=mk_qtTp):
                    S.wait(v, "pe", mk_qtTp)
                    i = v.tensor_copy(out=qtT[:], in_=ps_m1[:, 128:384])
                    S.inc(i, "dv")
                M[c][f"qtTv{h}"] = S.val("dv")

            def chainD(c, h, mk_stcur):
                @block.tensor
                def _(t, mk_w=max(M[c][f"qtTv{h}"], mk_stcur)):
                    S.wait(t, "dv", mk_w)
                    for mh in range(2):
                        i = t.matmul(out=ps_m1[:, 384:512],
                                     lhsT=stT[:, mh * 128:(mh + 1) * 128],
                                     rhs=qtT[:, mh * 128:(mh + 1) * 128],
                                     start=(mh == 0), stop=(mh == 1))
                    S.inc(i, "pe")
                mk_bt = S.val("pe")

                @block.vector
                def _(v, mk_bt=mk_bt):
                    S.wait(v, "pe", mk_bt)
                    i = v.tensor_tensor(out=btm[:], in0=ps_m1[:, 384:512], in1=utonesb[:],
                                        op=ALU.mult)
                    S.inc(i, "dv")
                M[c][f"btm{h}"] = S.val("dv")

            def chainE(c, h):
                sl2 = (c % 2) * 512
                hs = sl2 + h * 256
                rsl = ((c - 1) % 3) * 1024

                @block.tensor
                def _(t, h=h, hs=hs, mk_btm=M[c][f"btm{h}"], rsl=rsl):
                    S.wait(t, "dv", mk_btm)
                    for vh in range(2):
                        t.matmul(out=ps_i[:, 256 + vh * 128:256 + (vh + 1) * 128],
                                 lhsT=Vb[:, hs + vh * 128:hs + (vh + 1) * 128],
                                 rhs=btm[:], start=True, stop=False)
                        for mh in range(2):
                            i = t.matmul(out=ps_i[:, 256 + vh * 128:256 + (vh + 1) * 128],
                                         lhsT=s2b[:, rsl + h * 512 + mh * 256 + vh * 128:
                                                  rsl + h * 512 + mh * 256 + (vh + 1) * 128],
                                         rhs=qtT[:, mh * 128:(mh + 1) * 128],
                                         start=False, stop=(mh == 1))
                    S.inc(i, "pe")
                mk_ovp = S.val("pe")

                @block.vector
                def _(v, mk_ovp=mk_ovp):
                    S.wait(v, "pe", mk_ovp)
                    i = v.tensor_copy(out=eok[:], in_=ps_i[:, 256:512])
                    S.inc(i, "dv")
                mk_ove = S.val("dv")

                @block.scalar
                def _(sc, h=h, c=c, mk_ove=mk_ove):
                    S.wait(sc, "dv", mk_ove)
                    for vh in range(2):
                        i = sc.activation(
                            out=ov_sb[:, (2 * h + vh) * T + c * 128:
                                      (2 * h + vh) * T + (c + 1) * 128],
                            in_=eok[:, vh * 128:(vh + 1) * 128],
                            func=AF.Silu)
                    S.inc(i, "sc")

            emit_LN(0)
            for c in range(nch):
                if c + 1 < nch:
                    emit_LN(c + 1)
                if c >= 1:
                    chainA(c - 1, 0, M[c - 1]["ktb"])
                emit_A2a(c)
                if c >= 1:
                    chainB(c - 1, 0)
                if c == 0:
                    pref = f"d_g{hpg}_"
                    if pref + "ln" in dbg_aps:
                        @block.gpsimd
                        def _(g, pref=pref, mk_rv=M[0]["rv"]):
                            S.wait(g, "sc", mk_rv)
                            g.dma_start(out=dbg_aps[pref + "ln"], in_=lncol[:]).then_inc(S.h["qst"], 16)
                            g.dma_start(out=dbg_aps[pref + "ls"], in_=lnst[:]).then_inc(S.h["qst"], 16)
                        S.c["qst"] += 32
                    if pref + "st" in dbg_aps:
                        @block.gpsimd
                        def _(g, pref=pref, mk_st=M[0]["st"], mkq=S.val("qt0")):
                            S.wait(g, "dv", mk_st)
                            S.wait(g, "qt0", mkq)
                            g.dma_start(out=dbg_aps[pref + "xi"], in_=xin[:, 0:D]).then_inc(S.h["qst"], 16)
                            g.dma_start(out=dbg_aps[pref + "xn"], in_=xnbf[:, 0:D]).then_inc(S.h["qst"], 16)
                            g.dma_start(out=dbg_aps[pref + "xt"], in_=xnT[:, 0:1024]).then_inc(S.h["qst"], 16)
                            g.dma_start(out=dbg_aps[pref + "st"], in_=stb[:, 0:512]).then_inc(S.h["qst"], 16)
                            g.dma_start(out=dbg_aps[pref + "sp"], in_=spf[:]).then_inc(S.h["qst"], 16)
                            g.dma_start(out=dbg_aps[pref + "df"], in_=Df[:, 0:512]).then_inc(S.h["qst"], 16)
                            g.dma_start(out=dbg_aps[pref + "di"], in_=Dinv[:]).then_inc(S.h["qst"], 16)
                            g.dma_start(out=dbg_aps[pref + "kb"], in_=Kb[:]).then_inc(S.h["qst"], 16)
                            g.dma_start(out=dbg_aps[pref + "qt"], in_=QTb[:, 0:512]).then_inc(S.h["qst"], 16)
                        S.c["qst"] += 144
                if mix_stop and c == 0:
                    break
                if c >= 1:
                    stc = M[c]["ktb"]
                    chainC(c - 1, 0, stc)
                    chainD(c - 1, 0, stc)
                emit_B1_mm(c)
                if c >= 1:
                    chainE(c - 1, 0)
                    chainA(c - 1, 1, M[c - 1]["qtTv0"])
                    chainB(c - 1, 1)
                    chainC(c - 1, 1, stc)
                    chainD(c - 1, 1, stc)
                    chainE(c - 1, 1)
                emit_B1_vec(c)
            if mix_stop:
                mk_state = mk_state0h
            else:
                cl = nch - 1
                stc = M[cl]["ktb"]
                chainA(cl, 0, M[cl]["ktb"])
                chainB(cl, 0)
                chainC(cl, 0, stc)
                chainD(cl, 0, stc)
                chainE(cl, 0)
                chainA(cl, 1, M[cl]["qtTv0"])
                chainB(cl, 1)
                chainC(cl, 1, stc)
                chainD(cl, 1, stc)
                chainE(cl, 1)
                mk_state = M[cl]["state"]

            if mix_stop:
                continue
            @block.gpsimd
            def _(g, hpg=hpg):
                S.wait(g, "dv", S.val("dv"))
                S.wait(g, "sc", S.val("sc"))
                S.wait(g, "qst", S.val("qst"))
                g.dma_start(out=d_ccout.ap()[hpg * 512:(hpg + 1) * 512, :]
                            .rearrange("(g p) t -> p g t", p=128),
                            in_=ov_sb[:].rearrange("p (g t) -> p g t", g=4)
                            ).then_inc(S.h["qst"], 16)
            S.c["qst"] += 16
            dump_marks.append(S.val("qst"))


        if mix_stop:
            @block.gpsimd
            def _(g):
                S.wait(g, "qst", S.val("qst"))
                g.engine_nop()
            esB.close()
            return nc
        mk_mixer = mk_state
        mk_ccin = dump_marks[-1]

        if "d_cc" in dbg_aps:
            @block.gpsimd
            def _(g):
                S.wait(g, "qst", mk_ccin)
                g.dma_start(out=dbg_aps["d_cc"], in_=d_ccout.ap()[:, :]
                            ).then_inc(S.h["qst"], 16)
            S.c["qst"] += 16
        if "d_s1" in dbg_aps:
            @block.gpsimd
            def _(g):
                S.wait(g, "dv", mk_mixer)
                g.dma_start(out=dbg_aps["d_s1"], in_=s1f[:]).then_inc(S.h["qst"], 16)
                g.dma_start(out=dbg_aps["d_s2"], in_=s2f[:]).then_inc(S.h["qst"], 16)
            S.c["qst"] += 32

        if not do_ffn:
            @block.gpsimd
            def _(g):
                S.wait(g, "qst", S.val("qst"))
                g.engine_nop()
            esB.close()
            return nc

        esB.close()

        wslab = sb("wslab", [128, 4 * 4096], BF16)
        wdslab = sb("wdslab", [128, 2 * 1024], BF16)
        xqi8 = sb("xqi8", [128, D], I8)
        lncol = sb("lncol2", [128, 8], F32)
        lnst2 = sb("lnst2", [128, 12], F32)
        xqsb = sb("xqsb", [128, D], BF16)
        xqTg = sb("xqTg", [128, 8 * GTOK], BF16)
        xrb = sb("xrb", [128, 2 * D], F32)
        qcol = sb("qcol", [128, 16], F32)
        guT = sb("guT", [128, 32 * GTOK], F32)
        guqT = sb("guqT", [128, 32 * GTOK], I8)
        gcvt = sb("gcvt", [128, 2 * 256], BF16)
        sgT = sb("sgT", [128, 2 * GTOK], F32)
        sabs = sb("sabs", [128, GTOK], F32)
        mxM = [sb("mxM0", [128, GTOK], F32), sb("mxM1", [128, GTOK], F32)]
        mxT = sabs
        colmax = sb("colmax", [128, 4], F32)
        fcol = sb("fcol", [128, 16], F32)
        qrow = sb("qrow", [1, GTOK], F32R)
        ffn_g = sb("ffn_g", [128, 4 * D], F32)
        xrr = sb("xrr", [128, 2 * D], F32)
        ccA = sb("ccA", [128, 8 * 128], BF16)
        ccB = sb("ccB", [128, 8 * 128], BF16)
        osel = sb("osel", [128, 8 * 128], BF16)
        o2f = sb("o2f", [128, 8 * 128], F32)
        rmsT = sb("rmsT", [128, 8 * 128], BF16)
        rrow = sb("rrow", [1, 256], F32)
        outbuf = sb("outbuf", [128, D], F32)

        pd = [pst(f"pd{i}", [128, 1024]) for i in range(4)]

        st_marks = []
        xrr_marks = []
        def emit_xq(grp):
                xq_marks = []
                for bi in range(4):
                    blk = grp * 4 + bi
                    xpp = bi % 2

                    @block.gpsimd
                    def _(sy, blk=blk, bi=bi, grp=grp, xpp=xpp):
                        if grp == 0 and bi == 0:
                            S.wait(sy, "qst", mk_ccin)
                        if len(xq_marks) >= 2:
                            S.wait(sy, "qt0", xq_marks[-2])
                        S.wait(sy, f"qx{xpp}", S.val(f"qx{xpp}"))
                        sy.dma_start(out=xrb[:, xpp * D:(xpp + 1) * D],
                                     in_=ap_xr[blk * 128:(blk + 1) * 128, :]
                                     ).then_inc(S.h[f"qx{xpp}"], 16)
                    S.c[f"qx{xpp}"] += 16
                    mk_xr = S.val(f"qx{xpp}")

                    @block.vector
                    def _(v, bi=bi, mk_xr=mk_xr, xpp=xpp, grp=grp):
                        S.wait(v, f"qx{xpp}", mk_xr)
                        if grp == 0 and bi == 0:
                            S.wait(v, "qst", mk_ccin)
                        if xq_marks:
                            S.wait(v, "qt0", xq_marks[-1])
                        xb = xrb[:, xpp * D:(xpp + 1) * D]
                        v.bn_stats(out=lnst2[:, 0:6], in_=xb[:, 0:512])
                        i = v.bn_stats(out=lnst2[:, 6:12], in_=xb[:, 512:1024])
                        S.inc(i, "dv")
                        S.wait(v, "dv", S.val("dv"))
                        i = v.bn_aggr(out=lncol[:, 0:2], in_=lnst2[:, 0:12])
                        S.inc(i, "dv")
                        S.wait(v, "dv", S.val("dv"))
                        i = v.tensor_scalar_add(out=lncol[:, 2:3], in0=lncol[:, 1:2],
                                                  scalar1=1e-5)
                        S.guard(v, i, "dv")
                        v.reciprocal(out=lncol[:, 4:5], in_=lncol[:, 2:3])
                        v.tensor_reduce(out=lncol[:, 5:6], in_=xb, axis=AXX, op=ALU.max)
                        i = v.tensor_reduce(out=lncol[:, 6:7], in_=xb, axis=AXX, op=ALU.min)
                        S.inc(i, "dv")
                    mk_l = S.val("dv")

                    @block.scalar
                    def _(sc, mk_l=mk_l):
                        S.wait(sc, "dv", mk_l)
                        i = sc.activation(out=lncol[:, 3:4], in_=lncol[:, 4:5], func=AF.Sqrt)
                        S.inc(i, "sc")
                    mk_rv2 = S.val("sc")

                    @block.vector
                    def _(v, bi=bi, mk_rv2=mk_rv2, xpp=xpp):
                        S.wait(v, "sc", mk_rv2)
                        xb = xrb[:, xpp * D:(xpp + 1) * D]
                        g_ = lambda i: S.guard(v, i, "dv")
                        g_(v.tensor_tensor(out=lncol[:, 7:8], in0=lncol[:, 5:6],
                                           in1=lncol[:, 0:1], op=ALU.subtract))
                        g_(v.tensor_tensor(out=lncol[:, 4:5], in0=lncol[:, 0:1],
                                           in1=lncol[:, 6:7], op=ALU.subtract))
                        g_(v.tensor_tensor(out=lncol[:, 5:6], in0=lncol[:, 7:8],
                                           in1=lncol[:, 4:5], op=ALU.max))
                        g_(v.tensor_tensor(out=lncol[:, 6:7], in0=lncol[:, 5:6],
                                           in1=lncol[:, 3:4], op=ALU.mult))
                        g_(v.tensor_scalar_max(out=lncol[:, 5:6], in0=lncol[:, 6:7],
                                               scalar1=1e-5))
                        g_(v.reciprocal(out=lncol[:, 7:8], in_=lncol[:, 5:6]))
                        g_(v.tensor_scalar_mul(out=qcol[:, bi * 4 + 2:bi * 4 + 3],
                                               in0=lncol[:, 5:6], scalar1=1.0 / 127.0))
                        g_(v.scalar_tensor_tensor(out=qcol[:, bi * 4:bi * 4 + 1],
                                                  in0=lncol[:, 3:4], scalar=127.0,
                                                  in1=lncol[:, 7:8], op0=ALU.mult,
                                                  op1=ALU.mult))
                        g_(v.tensor_tensor(out=lncol[:, 6:7], in0=lncol[:, 0:1],
                                           in1=qcol[:, bi * 4:bi * 4 + 1], op=ALU.mult))
                        g_(v.tensor_scalar_mul(out=qcol[:, bi * 4 + 1:bi * 4 + 2],
                                               in0=lncol[:, 6:7], scalar1=-1.0))
                        g_(v.tensor_scalar(out=xqi8[:], in0=xb,
                                           scalar1=qcol[:, bi * 4:bi * 4 + 1],
                                           scalar2=qcol[:, bi * 4 + 1:bi * 4 + 2],
                                           op0=ALU.mult, op1=ALU.add))
                        i = v.tensor_scalar_mul(out=xqsb[:], in0=xqi8[:],
                                                scalar1=qcol[:, bi * 4 + 2:bi * 4 + 3])
                        S.inc(i, "dv")
                    mk_xq = S.val("dv")

                    if grp == 0 and bi == 0 and "d_xq" in dbg_aps:
                        @block.gpsimd
                        def _(g, mk_xq=mk_xq):
                            S.wait(g, "dv", mk_xq)
                            g.dma_start(out=dbg_aps["d_xq"], in_=xqsb[:]
                                        ).then_inc(S.h["qst"], 16)
                        S.c["qst"] += 16

                    @block.sync
                    def _(sy, bi=bi, mk_xq=mk_xq):
                        S.wait(sy, "dv", mk_xq)
                        S.wait(sy, "qt0", S.val("qt0"))
                        sy.dma_start(
                            out=xqTg[:].rearrange("p (b f) -> p b f", f=GTOK)
                            [:, :, bi * 128:(bi + 1) * 128],
                            in_=xqsb[:].rearrange("p (b f) -> p b f", b=8),
                            transpose=True).then_inc(S.h["qt0"], 16)
                    S.c["qt0"] += 16
                    xq_marks.append(S.val("qt0"))
                mk_xqT = S.val("qt0")
                return mk_xqT

        for grp in range(NGRP):
            mk_xqT = emit_xq(grp) if grp == 0 else pending_xqT

            @block.vector
            def _(v):
                i = v.memset(mxM[0][:], 0.0)
                S.inc(i, "dv")
            mk_mx0 = S.val("dv")

            slab_marks = []
            prod_marks = []
            absmax_marks = []
            for ic in range(8):
                pp = ic % 2

                @block.gpsimd
                def _(g, ic=ic, pp=pp, grp=grp):
                    if len(slab_marks) >= 2:
                        S.wait(g, "pe", slab_marks[-2])
                    elif grp > 0:
                        S.wait(g, "pe", grp_slab_tail[pp])
                    base = S.val(f"qg{pp}")
                    for widx, dscr in ((0, d_wgq), (1, d_wuq)):
                        if widx == 0 and grp == 0 and ic == 0:
                            S.wait(g, "qst", mk_ccin)
                            S.wait(g, "qs0", mk_wq0)
                            S.wait(g, "qs1", mk_wq1)
                        g.wait_ge(S.h[f"qg{pp}"], base + widx * 16)
                        g.dma_start(
                            out=wslab[:, (pp * 2 + widx) * 4096:(pp * 2 + widx + 1) * 4096]
                            .rearrange("p (b f) -> p b f", b=8),
                            in_=dscr.rearrange("(b p) i -> p b i", p=128)
                            [:, :, ic * 512:(ic + 1) * 512]).then_inc(S.h[f"qg{pp}"], 16)
                S.c[f"qg{pp}"] += 32
                mk_slab = S.val(f"qg{pp}")

                for isl in range(4):
                    sl_id = ic * 4 + isl

                    pdg = pd[sl_id % 2]
                    sgs = (sl_id % 2) * GTOK

                    @block.tensor
                    def _(t, pp=pp, isl=isl, mk_slab=mk_slab, mk_xqT=mk_xqT,
                          sl_id=sl_id, pdg=pdg):
                        S.wait(t, f"qg{pp}", mk_slab)
                        S.wait(t, "qt0", mk_xqT)
                        if len(prod_marks) >= 2:
                            S.wait(t, "dv", prod_marks[-2])
                        gbase = (pp * 2 + 0) * 4096
                        ubase = (pp * 2 + 1) * 4096
                        for dc in range(8):
                            i = t.matmul(out=pdg[:, 0:512],
                                         lhsT=wslab[:, gbase + dc * 512 + isl * 128:
                                                    gbase + dc * 512 + (isl + 1) * 128],
                                         rhs=xqTg[:, dc * GTOK:(dc + 1) * GTOK],
                                         start=(dc == 0), stop=(dc == 7))
                        for dc in range(8):
                            i = t.matmul(out=pdg[:, 512:1024],
                                         lhsT=wslab[:, ubase + dc * 512 + isl * 128:
                                                    ubase + dc * 512 + (isl + 1) * 128],
                                         rhs=xqTg[:, dc * GTOK:(dc + 1) * GTOK],
                                         start=(dc == 0), stop=(dc == 7))
                        S.inc(i, "pe")
                    slab_marks.append(S.val("pe"))
                    mk_gu = S.val("pe")

                    @block.scalar
                    def _(sc, mk_gu=mk_gu, pdg=pdg, sgs=sgs):
                        S.wait(sc, "pe", mk_gu)
                        i = sc.activation(out=sgT[:, sgs:sgs + GTOK], in_=pdg[:, 0:512],
                                          func=AF.Silu, scale=inv_swg[:])
                        S.inc(i, "sc")
                    mk_sg = S.val("sc")

                    if grp == 0 and sl_id == 0 and "d_sg" in dbg_aps:
                        @block.gpsimd
                        def _(g, mk_sg=mk_sg):
                            S.wait(g, "sc", mk_sg)
                            g.dma_start(out=dbg_aps["d_sg"], in_=sgT[:, 0:GTOK]
                                        ).then_inc(S.h["qst"], 16)
                        S.c["qst"] += 16

                    @block.vector
                    def _(v, sl_id=sl_id, mk_sg=mk_sg, pdg=pdg, sgs=sgs):
                        S.wait(v, "sc", mk_sg)
                        if grp == 0 and sl_id == 0 and "d_sg" in dbg_aps:
                            S.wait(v, "qst", S.val("qst"))
                        i = v.tensor_tensor(out=guT[:, sl_id * GTOK:(sl_id + 1) * GTOK],
                                            in0=pdg[:, 512:1024], in1=sgT[:, sgs:sgs + GTOK],
                                            op=ALU.mult)
                        S.inc(i, "dv")
                    prod_marks.append(S.val("dv"))
                    if grp == 0 and sl_id == 0 and "d_gu" in dbg_aps:
                        @block.gpsimd
                        def _(g, m=prod_marks[-1]):
                            S.wait(g, "dv", m)
                            g.dma_start(out=dbg_aps["d_gu"], in_=guT[:, 0:GTOK]
                                        ).then_inc(S.h["qst"], 16)
                        S.c["qst"] += 16
                    mk_prod1 = prod_marks[-1]

                    @block.vector
                    def _(v, sl_id=sl_id, mk_prod1=mk_prod1, mk_mx0=mk_mx0):
                        S.wait(v, "dv", max(mk_prod1, mk_mx0))
                        a, b = sl_id % 2, 1 - sl_id % 2
                        gus = guT[:, sl_id * GTOK:(sl_id + 1) * GTOK]
                        i = v.tensor_tensor(out=sabs[:], in0=gus, in1=mxM[a][:],
                                            op=ALU.max)
                        S.guard(v, i, "dv")
                        i = v.scalar_tensor_tensor(out=mxM[b][:], in0=gus, scalar=-1.0,
                                                   in1=sabs[:], op0=ALU.mult, op1=ALU.max)
                        S.inc(i, "dv")
                    absmax_marks.append(S.val("dv"))
            grp_slab_tail = {0: slab_marks[-2], 1: slab_marks[-1]}
            mk_absmax = absmax_marks[-1]

            @block.tensor
            def _(t, mk_absmax=mk_absmax):
                S.wait(t, "dv", mk_absmax)
                for bi in range(4):
                    i = t.matmul(out=pd[1][:, bi * 128:(bi + 1) * 128],
                                 lhsT=mxM[0][:, bi * 128:(bi + 1) * 128],
                                 rhs=identf[:], start=True, stop=True)
                S.inc(i, "pe")
            mk_mxT = S.val("pe")

            @block.vector
            def _(v, mk_mxT=mk_mxT):
                S.wait(v, "pe", mk_mxT)
                i = v.tensor_copy(out=mxT[:], in_=pd[1][:, 0:512])
                S.guard(v, i, "dv")
                for bi in range(4):
                    i = v.tensor_reduce(out=colmax[:, bi:bi + 1],
                                        in_=mxT[:, bi * 128:(bi + 1) * 128], axis=AXX,
                                        op=ALU.max)
                S.guard(v, i, "dv")
                g_ = lambda i: S.guard(v, i, "dv")
                for bi in range(4):
                    g_(v.tensor_scalar_mul(out=fcol[:, bi * 4 + 3:bi * 4 + 4],
                                           in0=colmax[:, bi:bi + 1], scalar1=inv_swu[:]))
                    g_(v.tensor_scalar(out=fcol[:, bi * 4:bi * 4 + 1],
                                       in0=fcol[:, bi * 4 + 3:bi * 4 + 4],
                                       scalar1=1e-5, scalar2=None, op0=ALU.max))
                    g_(v.reciprocal(out=fcol[:, bi * 4 + 1:bi * 4 + 2],
                                    in_=fcol[:, bi * 4:bi * 4 + 1]))
                    g_(v.scalar_tensor_tensor(out=fcol[:, bi * 4 + 2:bi * 4 + 3],
                                              in0=fcol[:, bi * 4 + 1:bi * 4 + 2],
                                              scalar=127.0, in1=inv_swu[:],
                                              op0=ALU.mult, op1=ALU.mult))
                    i = v.scalar_tensor_tensor(out=fcol[:, bi * 4 + 3:bi * 4 + 4],
                                               in0=fcol[:, bi * 4:bi * 4 + 1],
                                               scalar=1.0 / 127.0, in1=inv_swd[:],
                                               op0=ALU.mult, op1=ALU.mult)
                    if bi < 3:
                        g_(i)
                S.inc(i, "dv")
            mk_fcol = S.val("dv")

            @block.tensor
            def _(t, mk_fcol=mk_fcol):
                S.wait(t, "dv", mk_fcol)
                for bi in range(4):
                    i = t.matmul(out=pd[1][0:1, 512 + bi * 128:512 + (bi + 1) * 128],
                                 lhsT=fcol[:, bi * 4 + 2:bi * 4 + 3], rhs=identf[:],
                                 start=True, stop=True)
                S.inc(i, "pe")
            mk_qrowp = S.val("pe")

            @block.vector
            def _(v, mk_qrowp=mk_qrowp):
                S.wait(v, "pe", mk_qrowp)
                i = v.tensor_copy(out=qrow[:], in_=pd[1][0:1, 512:1024])
                S.inc(i, "dv")
            mk_qrow = S.val("dv")

            @block.tensor
            def _(t, mk_qrow=mk_qrow):
                S.wait(t, "dv", mk_qrow)
                i = t.matmul(out=pd[1][:, 0:512], lhsT=ones_row[:], rhs=qrow[:],
                             start=True, stop=True)
                S.inc(i, "pe")
            mk_qrep = S.val("pe")

            guq_marks = []

            @block.vector
            def _(v, mk_qrep=mk_qrep):
                S.wait(v, "pe", mk_qrep)
                for sl_id in range(32):
                    i = v.tensor_tensor(out=guqT[:, sl_id * GTOK:(sl_id + 1) * GTOK],
                                        in0=guT[:, sl_id * GTOK:(sl_id + 1) * GTOK],
                                        in1=pd[1][:, 0:512], op=ALU.mult)
                    S.inc(i, "dv")
                    guq_marks.append(S.val("dv") + sl_id)
            guq_marks = [S.val("dv") - 32 + 1 + k for k in range(32)]
            mk_guq = S.val("dv")

            if grp == 0 and "d_fc" in dbg_aps:
                @block.gpsimd
                def _(g, mk_fcol=mk_fcol):
                    S.wait(g, "dv", mk_fcol)
                    g.dma_start(out=dbg_aps["d_fc"], in_=fcol[:]).then_inc(S.h["qst"], 16)
                S.c["qst"] += 16
            if grp == 0 and "d_guq" in dbg_aps:
                @block.gpsimd
                def _(g, mk_guq=mk_guq):
                    S.wait(g, "dv", mk_guq)
                    g.dma_start(out=dbg_aps["d_guq"], in_=guqT[:, 0:GTOK]
                                ).then_inc(S.h["qst"], 16)
                S.c["qst"] += 16

            dq_done = []
            wd_marks = []
            for sg in range(2):
                for dc in range(32):
                    ppd = dc % 2

                    @block.gpsimd
                    def _(g, dc=dc, ppd=ppd):
                        if len(wd_marks) >= 2:
                            S.wait(g, "pe", wd_marks[-2])
                        S.wait(g, f"qd{ppd}", S.val(f"qd{ppd}"))
                        g.dma_start(out=wdslab[:, ppd * 1024:(ppd + 1) * 1024],
                                    in_=d_wdq[dc * 128:(dc + 1) * 128, :]
                                    ).then_inc(S.h[f"qd{ppd}"], 16)
                    S.c[f"qd{ppd}"] += 16
                    mk_wd = S.val(f"qd{ppd}")

                    @block.vector
                    def _(v, dc=dc, sg=sg, ppd=ppd, mk_g=guq_marks[dc]):
                        S.wait(v, "dv", mk_g)
                        if len(wd_marks) >= 2:
                            S.wait(v, "pe", wd_marks[-2])
                        i = v.tensor_copy(out=gcvt[:, ppd * 256:(ppd + 1) * 256],
                                          in_=guqT[:, dc * GTOK + sg * 256:
                                                   dc * GTOK + (sg + 1) * 256])
                        S.inc(i, "dv")
                    mk_cvt = S.val("dv")

                    @block.tensor
                    def _(t, dc=dc, sg=sg, ppd=ppd, mk_wd=mk_wd, mk_cvt=mk_cvt):
                        S.wait(t, f"qd{ppd}", mk_wd)
                        S.wait(t, "dv", mk_cvt)
                        for bi2 in range(2):
                            for half in range(2):
                                i = t.matmul(
                                    out=pd[sg * 2 + bi2][:, half * 512:(half + 1) * 512],
                                    lhsT=gcvt[:, ppd * 256 + bi2 * 128:
                                              ppd * 256 + (bi2 + 1) * 128],
                                    rhs=wdslab[:, ppd * 1024 + half * 512:
                                               ppd * 1024 + (half + 1) * 512],
                                    start=(dc == 0), stop=(dc == 31))
                        S.inc(i, "pe")
                    wd_marks.append(S.val("pe"))
                dq_done.append(S.val("pe"))

                for bi2 in range(2):
                    bi = sg * 2 + bi2
                    blk2 = grp * 4 + bi

                    @block.sync
                    def _(sy, blk2=blk2, bi2=bi2, m=xrr_marks[-1] if False else 0):
                        if xrr_marks:
                            S.wait(sy, "dv", xrr_marks[-1])
                        S.wait(sy, "qv", S.val("qv"))
                        sy.dma_start(out=xrr[:, bi2 * D:(bi2 + 1) * D],
                                     in_=ap_xr[blk2 * 128:(blk2 + 1) * 128, :]
                                     ).then_inc(S.h["qv"], 16)
                    S.c["qv"] += 16
                    mk_xrr = S.val("qv")

                    @block.vector
                    def _(v, bi=bi, bi2=bi2, m=dq_done[-1], mk_xrr=mk_xrr):
                        S.wait(v, "pe", m)
                        S.wait(v, "qv", mk_xrr)
                        i = v.scalar_tensor_tensor(
                            out=ffn_g[:, bi * D:(bi + 1) * D], in0=pd[bi][:],
                            scalar=fcol[:, bi * 4 + 3:bi * 4 + 4],
                            in1=xrr[:, bi2 * D:(bi2 + 1) * D], op0=ALU.mult, op1=ALU.add)
                        S.inc(i, "dv")
                    xrr_marks.append(S.val("dv"))
            mk_ffn = S.val("dv")
            if grp == 0 and "d_ffn" in dbg_aps:
                @block.gpsimd
                def _(g, mk_ffn=mk_ffn):
                    S.wait(g, "dv", mk_ffn)
                    g.dma_start(out=dbg_aps["d_ffn"], in_=ffn_g[:]).then_inc(S.h["qst"], 16)
                S.c["qst"] += 16

            if grp + 1 < NGRP:
                pending_xqT = emit_xq(grp + 1)

            for bi in range(4):
                blk = grp * 4 + bi

                @block.sync
                def _(sy, blk=blk, bi=bi, mk_ffn=mk_ffn):
                    if grp == 0 and bi == 0:
                        S.wait(sy, "qst", mk_ccin)
                    S.wait(sy, "dv", mk_ffn)
                    if st_marks:
                        S.wait(sy, "qst", st_marks[-1])
                    S.wait(sy, "qcc", S.val("qcc"))
                    S.wait(sy, "qcb", S.val("qcb"))
                    sy.dma_start(out=ccA[:].rearrange("p (b f) -> p b f", b=8),
                                 in_=d_ccout.ap().rearrange("(b p) t -> p b t", p=128)
                                 [:, :, blk * 128:(blk + 1) * 128]
                                 ).then_inc(S.h["qcc"], 16)
                    sy.dma_start(out=ccB[:].rearrange("p (b f) -> p b f", b=8),
                                 in_=d_ccout.ap().rearrange("(b p) t -> p b t", p=128)
                                 [:, :, TOWN + blk * 128:TOWN + (blk + 1) * 128]
                                 ).then_inc(S.h["qcb"], 16)
                S.c["qcc"] += 16
                S.c["qcb"] += 16
                mk_cc = S.val("qcc")
                mk_cb = S.val("qcb")

                @block.vector
                def _(v, mk_cc=mk_cc, mk_cb=mk_cb):
                    S.wait(v, "qcc", mk_cc)
                    S.wait(v, "qcb", mk_cb)
                    i = v.tensor_tensor(out=o2f[:], in0=ccB[:], in1=ccA[:],
                                        op=ALU.subtract)
                    S.guard(v, i, "dv")
                    i = v.scalar_tensor_tensor(out=osel[:], in0=o2f[:], scalar=hpcol[:],
                                               in1=ccA[:], op0=ALU.mult, op1=ALU.add)
                    S.guard(v, i, "dv")
                    i = v.tensor_tensor(out=o2f[:], in0=osel[:], in1=osel[:],
                                        op=ALU.mult)
                    S.inc(i, "dv")
                mk_osel = S.val("dv")
                if grp == 0 and bi == 0 and "d_cca" in dbg_aps:
                    @block.gpsimd
                    def _(g, mk_osel=mk_osel):
                        S.wait(g, "dv", mk_osel)
                        g.dma_start(out=dbg_aps["d_cca"], in_=ccA[:]).then_inc(S.h["qst"], 16)
                        g.dma_start(out=dbg_aps["d_osel"], in_=osel[:]).then_inc(S.h["qst"], 16)
                    S.c["qst"] += 32

                mk_o2 = mk_osel

                @block.tensor
                def _(t, mk_o2=mk_o2, mk_ffn=mk_ffn):
                    S.wait(t, "dv", mk_o2)
                    S.wait(t, "dv", mk_ffn)
                    for cht in range(8):
                        i = t.matmul(out=pd[0][0:1, 0:128],
                                     lhsT=ones_col[:].bitcast(F32),
                                     rhs=o2f[:, cht * 128:(cht + 1) * 128],
                                     start=(cht == 0), stop=(cht == 7))
                    S.inc(i, "pe")
                mk_ssq = S.val("pe")

                @block.vector
                def _(v, mk_ssq=mk_ssq):
                    S.wait(v, "pe", mk_ssq)
                    i = v.tensor_scalar(out=rrow[0:1, 0:128], in0=pd[0][0:1, 0:128],
                                        scalar1=1.0 / D, scalar2=1e-5,
                                        op0=ALU.mult, op1=ALU.add)
                    S.inc(i, "dv")
                mk_r1 = S.val("dv")

                @block.scalar
                def _(sc, mk_r1=mk_r1):
                    S.wait(sc, "dv", mk_r1)
                    i = sc.activation(out=rrow[0:1, 128:256], in_=rrow[0:1, 0:128],
                                      func=AF.Ln)
                    S.inc(i, "sc")
                    S.wait(sc, "sc", S.val("sc"))
                    i = sc.activation(out=rrow[0:1, 0:128], in_=rrow[0:1, 128:256],
                                      func=AF.Exp, scale=-0.5)
                    S.inc(i, "sc")
                mk_rs = S.val("sc")

                @block.tensor
                def _(t, mk_rs=mk_rs):
                    S.wait(t, "sc", mk_rs)
                    i = t.matmul(out=pd[0][:, 128:256],
                                 lhsT=ones_row[:].bitcast(F32),
                                 rhs=rrow[0:1, 0:128], start=True, stop=True)
                    S.inc(i, "pe")
                mk_rrep = S.val("pe")

                @block.vector
                def _(v, mk_rrep=mk_rrep):
                    S.wait(v, "pe", mk_rrep)
                    for cht in range(8):
                        i = v.tensor_tensor(out=rmsT[:, cht * 128:(cht + 1) * 128],
                                            in0=osel[:, cht * 128:(cht + 1) * 128],
                                            in1=pd[0][:, 128:256], op=ALU.mult)
                    S.inc(i, "dv")
                mk_rmsT = S.val("dv")

                @block.tensor
                def _(t, mk_rmsT=mk_rmsT, mk_woTg=mk_woTg):
                    S.wait(t, "dv", max(mk_rmsT, mk_woTg))
                    for half in range(2):
                        for cht in range(8):
                            i = t.matmul(out=pd[1][:, half * 512:(half + 1) * 512],
                                         lhsT=rmsT[:, cht * 128:(cht + 1) * 128],
                                         rhs=woTg[:, cht * 1024 + half * 512:
                                                  cht * 1024 + (half + 1) * 512],
                                         start=(cht == 0), stop=(cht == 7))
                    S.inc(i, "pe")
                mk_attn = S.val("pe")

                @block.vector
                def _(v, bi=bi, mk_attn=mk_attn):
                    S.wait(v, "pe", mk_attn)
                    i = v.tensor_tensor(out=outbuf[:], in0=pd[1][:],
                                        in1=ffn_g[:, bi * D:(bi + 1) * D], op=ALU.add)
                    S.inc(i, "dv")
                mk_out = S.val("dv")

                @block.gpsimd
                def _(g, blk=blk, mk_out=mk_out):
                    S.wait(g, "dv", mk_out)
                    S.wait(g, "qst", S.val("qst"))
                    g.dma_start(out=ap_out[blk * 128:(blk + 1) * 128, :],
                                in_=outbuf[:]).then_inc(S.h["qst"], 16)
                S.c["qst"] += 16
                st_marks.append(S.val("qst"))

        @block.gpsimd
        def _(g):
            S.wait(g, "qst", S.val("qst"))
            g.engine_nop()

    return nc


_NC_CACHE = {}


def _get_nc():
    if "nc" not in _NC_CACHE:
        _NC_CACHE["nc"] = build_nc()
    return _NC_CACHE["nc"]


def kernel(L0_attn_hidden_states, L0_attn_Wq, L0_attn_Wk, L0_attn_Wv, L0_attn_Wf,
           L0_attn_g_w, L0_attn_Wo, L0_ffn_w_gate, L0_ffn_w_up, L0_ffn_w_down):
    x = np.ascontiguousarray(np.asarray(L0_attn_hidden_states, dtype=np.float32))
    Wq = np.asarray(L0_attn_Wq, dtype=np.float32)
    Wk = np.asarray(L0_attn_Wk, dtype=np.float32)
    Wv = np.asarray(L0_attn_Wv, dtype=np.float32)
    Wf = np.asarray(L0_attn_Wf, dtype=np.float32)
    gw = np.ascontiguousarray(np.asarray(L0_attn_g_w, dtype=np.float32).reshape(8, 128).T)
    woT = np.ascontiguousarray(np.asarray(L0_attn_Wo, dtype=np.float32).T)
    wgT = np.ascontiguousarray(np.asarray(L0_ffn_w_gate, dtype=np.float32).T)
    wuT = np.ascontiguousarray(np.asarray(L0_ffn_w_up, dtype=np.float32).T)
    wdT = np.ascontiguousarray(np.asarray(L0_ffn_w_down, dtype=np.float32).T)

    wqT = np.ascontiguousarray(Wq.T)
    wkT = np.ascontiguousarray(Wk.T)
    wvT = np.ascontiguousarray(Wv.T)
    wfT = np.ascontiguousarray(Wf.T)

    in_maps = []
    for c in range(8):
        b, hp = c // 2, c % 2
        in_maps.append({
            "x": x[b],
            "xr": np.ascontiguousarray(x[b, hp * TOWN:(hp + 1) * TOWN]),
            "wqT": wqT, "wkT": wkT, "wvT": wvT, "wfT": wfT,
            "woT": woT, "gw": gw,
            "wgT": wgT, "wuT": wuT, "wdT": wdT,
            "hpc": np.full((128, 1), float(hp), np.float32),
        })

    nc = _get_nc()
    res = run_bass_kernel_spmd(nc, in_maps, core_ids=list(range(8)))
    out = np.empty((4, T, D), np.float32)
    for c in range(8):
        b, hp = c // 2, c % 2
        out[b, hp * TOWN:(hp + 1) * TOWN] = res.results[c]["out"]
    return out


# version 24 (final)
# speedup vs baseline: 1.0069x; 1.0069x over previous
import numpy as np
from contextlib import ExitStack

import concourse.bass as bass
import concourse.mybir as mybir
from concourse.bass_utils import run_bass_kernel_spmd

F32 = mybir.dt.float32
F32R = mybir.dt.float32r
BF16 = mybir.dt.bfloat16
I8 = mybir.dt.int8
AF = mybir.ActivationFunctionType
ALU = mybir.AluOpType
AXX = mybir.AxisListType.X

T = 4096
D = 1024
C = 128
NCH = T // C
TOWN = 2048
NBLK = TOWN // C
INNER = 4096
NGRP = 4
GTOK = 512


class Sems:
    def __init__(self, es, nc, names):
        self.h = {n: es.enter_context(nc.semaphore(n)) for n in names}
        self.c = {n: 0 for n in names}

    def inc(self, instr, name, n=1):
        instr.then_inc(self.h[name], n)
        self.c[name] += n
        return self.c[name]

    def val(self, name):
        return self.c[name]

    def wait(self, eng, name, val):
        if val > 0:
            eng.wait_ge(self.h[name], val)

    def guard(self, eng, instr, name):
        self.inc(instr, name)
        eng.wait_ge(self.h[name], self.c[name])


def build_nc(nch=NCH, do_ffn=True, do_ag=True, dbg=(), mix_stop=0):
    nc = bass.Bass("TRN2", target_bir_lowering=False, debug=False,
                   detect_race_conditions=False)

    ap_x = nc.dram_tensor("x", [T, D], F32, kind="ExternalInput").ap()
    ap_xr = nc.dram_tensor("xr", [TOWN, D], F32, kind="ExternalInput").ap()
    ap_wqT = nc.dram_tensor("wqT", [D, D], F32, kind="ExternalInput").ap()
    ap_wkT = nc.dram_tensor("wkT", [D, D], F32, kind="ExternalInput").ap()
    ap_wvT = nc.dram_tensor("wvT", [D, D], F32, kind="ExternalInput").ap()
    ap_wfT = nc.dram_tensor("wfT", [D, D], F32, kind="ExternalInput").ap()
    ap_woT = nc.dram_tensor("woT", [D, D], F32, kind="ExternalInput").ap()
    ap_gw = nc.dram_tensor("gw", [128, 8], F32, kind="ExternalInput").ap()
    ap_wgT = nc.dram_tensor("wgT", [D, INNER], F32, kind="ExternalInput").ap()
    ap_wuT = nc.dram_tensor("wuT", [D, INNER], F32, kind="ExternalInput").ap()
    ap_wdT = nc.dram_tensor("wdT", [INNER, D], F32, kind="ExternalInput").ap()
    ap_hp = nc.dram_tensor("hpc", [128, 1], F32, kind="ExternalInput").ap()
    ap_out = nc.dram_tensor("out", [TOWN, D], F32, kind="ExternalOutput").ap()

    d_wgq = nc.dram_tensor("wgq_s", [D, INNER], BF16).ap()
    d_wuq = nc.dram_tensor("wuq_s", [D, INNER], BF16).ap()
    d_wdq = nc.dram_tensor("wdq_s", [INNER, D], BF16).ap()
    d_ccout = nc.dram_tensor("cc_out", [1024, T], BF16)

    dbg_aps = {}
    for name, shape in dict(dbg).items():
        dbg_aps[name] = nc.dram_tensor(name, shape, F32, kind="ExternalOutput").ap()

    es = ExitStack()
    sb = lambda n, s, dt=F32: es.enter_context(nc.sbuf_tensor(n, s, dt))
    pst = lambda n, s: es.enter_context(nc.psum_tensor(n, s, F32))

    with es:
        S = Sems(es, nc, ["qx0", "qx1", "qt0", "qt1", "qg0", "qg1", "qd0", "qd1",
                          "qw", "qst", "qcc", "qcb", "qv",
                          "qa0", "qa1", "qs0", "qs1",
                          "pe", "dv", "sc", "gp", "cs", "mk"])

        identb = sb("identb", [128, 128], BF16)
        identf = sb("identf", [128, 128], F32)
        utones = sb("utones", [128, 128], F32R)
        utonesb = sb("utonesb", [128, 128], BF16)
        ones_row = sb("ones_row", [1, 128], F32R)
        ones_col = sb("ones_col", [128, 1], F32R)
        one_one = sb("one_one", [1, 1], F32R)
        hpcol = sb("hpcol", [128, 1], F32)
        gwcol = sb("gwcol", [128, 8], F32)
        woTg = sb("woTg", [128, 8 * 1024], BF16)
        inv_swg = sb("inv_swg", [128, 1], F32)
        inv_swu = sb("inv_swu", [128, 1], F32)
        inv_swd = sb("inv_swd", [128, 1], F32)

        block = es.enter_context(nc.Block())

        esB = ExitStack()
        sbB = lambda n, s, dt=F32: esB.enter_context(nc.sbuf_tensor(n, s, dt))
        pstB = lambda n, s: esB.enter_context(nc.psum_tensor(n, s, F32))

        esC = ExitStack()
        cscr = esC.enter_context(nc.sbuf_tensor("cscr", [128, 128], F32))

        @block.gpsimd
        def _(g):
            g.memset(identb[:], 0.0)
            g.memset(identf[:], 0.0)
            i = g.memset(cscr[:], 0.0)
            S.inc(i, "mk")
            S.wait(g, "mk", S.val("mk"))
            g.affine_select(out=identb[:], in_=identb[:], compare_op=ALU.not_equal,
                            fill=1.0, base=0, pattern=[[-1, 128]], channel_multiplier=1)
            g.affine_select(out=identf[:], in_=identf[:], compare_op=ALU.not_equal,
                            fill=1.0, base=0, pattern=[[-1, 128]], channel_multiplier=1)
            i = g.affine_select(out=cscr[:], in_=cscr[:], compare_op=ALU.is_gt,
                                fill=1.0, base=0, pattern=[[-1, 128]], channel_multiplier=1)
            S.inc(i, "gp")
        mk_masks0 = S.val("gp")

        @block.vector
        def _(v, mk_masks0=mk_masks0):
            S.wait(v, "gp", mk_masks0)
            v.tensor_copy(out=utones[:], in_=cscr[:])
            i = v.tensor_copy(out=utonesb[:], in_=cscr[:])
            S.guard(v, i, "dv")
            i = v.memset(cscr[:, 0:1], 1.0)
            S.guard(v, i, "dv")
            i = v.memset(cscr[0:1, 0:128], 1.0)
            S.guard(v, i, "dv")
            v.tensor_copy(out=ones_col[:], in_=cscr[:, 0:1])
            v.tensor_copy(out=ones_row[:], in_=cscr[0:1, 0:128])
            i = v.tensor_copy(out=one_one[:], in_=cscr[0:1, 0:1])
            S.inc(i, "dv")
        mk_consts = S.val("dv")

        @block.gpsimd
        def _(g, mk_consts=mk_consts):
            S.wait(g, "dv", mk_consts)
            i = g.engine_nop()
            S.inc(i, "gp")
        mk_masks = S.val("gp")
        esC.close()

        wq_b = sbB("wq_b", [128, 8 * 512], BF16)
        wk_b = sbB("wk_b", [128, 8 * 512], BF16)
        wv_b = sbB("wv_b", [128, 8 * 512], BF16)
        wf_b = sbB("wf_b", [128, 8 * 512], BF16)
        woT_f = sbB("woT_f", [128, 8 * 1024], F32)

        @block.gpsimd
        def _(g):
            g.dma_start(out=hpcol[:], in_=ap_hp[:, :]).then_inc(S.h["qw"], 16)
            g.wait_ge(S.h["qw"], S.c["qw"] + 16)
            g.dma_start(out=gwcol[:], in_=ap_gw[:, :]).then_inc(S.h["qw"], 16)
            g.wait_ge(S.h["qw"], S.c["qw"] + 32)
            g.dma_start(out=woT_f[:].rearrange("p (b f) -> p b f", b=8),
                        in_=ap_woT.rearrange("(b p) f -> p b f", p=128)
                        ).then_inc(S.h["qw"], 16)
        S.c["qw"] += 3 * 16
        mk_wload = S.val("qw")

        @block.vector
        def _(v):
            S.wait(v, "qw", mk_wload)
            for cc in range(8):
                i = v.tensor_scalar_mul(out=woTg[:, cc * 1024:(cc + 1) * 1024],
                                        in0=woT_f[:, cc * 1024:(cc + 1) * 1024],
                                        scalar1=gwcol[:, cc:cc + 1])
            S.inc(i, "dv")
        mk_woTg = S.val("dv")

        if do_ffn:
            esW = ExitStack()
            sbW = lambda n, s, dt=F32: esW.enter_context(nc.sbuf_tensor(n, s, dt))
            wtmp0 = sbW("wtmp0", [128, 4096], F32)
            wtmp1 = sbW("wtmp1", [128, 4096], F32)
            wi8 = sbW("wi8", [128, 4096], I8)
            wqb0 = sbW("wqb0", [128, 4096], BF16)
            wqb1 = sbW("wqb1", [128, 4096], BF16)
            mpart = sbW("mpart", [128, 32], F32)
            mcol1 = sbW("mcol1", [128, 1], F32)
            sc11 = sbW("sc11", [1, 4], F32)
            swcol = sbW("swcol", [128, 1], F32)
            ps_w = esW.enter_context(nc.psum_tensor("ps_w", [128, 512], F32))

            q_marks = [0, 0]
            st_tail = {0: 0, 1: 0}
            for apw, dscr, ntile, tw, invcol in (
                    (ap_wgT, d_wgq, 8, 4096, inv_swg),
                    (ap_wuT, d_wuq, 8, 4096, inv_swu),
                    (ap_wdT, d_wdq, 32, 1024, inv_swd)):
                red_marks = [max(q_marks), max(q_marks)]
                for it in range(ntile):
                    p = it % 2
                    buf = wtmp0 if p == 0 else wtmp1

                    @block.sync
                    def _(sy, it=it, apw=apw, tw=tw, buf=buf, p=p,
                          mk_free=red_marks[it]):
                        S.wait(sy, "dv", mk_free)
                        S.wait(sy, f"qa{p}", S.val(f"qa{p}"))
                        sy.dma_start(out=buf[:, :tw],
                                     in_=apw[it * 128:(it + 1) * 128, :]
                                     ).then_inc(S.h[f"qa{p}"], 16)
                    S.c[f"qa{p}"] += 16
                    mk = S.val(f"qa{p}")

                    @block.vector
                    def _(v, it=it, tw=tw, mk=mk, buf=buf, p=p):
                        S.wait(v, f"qa{p}", mk)
                        i = v.reduce_sum(out=mpart[:, it:it + 1], in_=buf[:, :tw],
                                         axis=AXX, apply_absolute_value=True)
                        S.inc(i, "dv")
                    red_marks.append(S.val("dv"))

                @block.vector
                def _(v, ntile=ntile, m=red_marks[-1]):
                    S.wait(v, "dv", m)
                    with nc.allow_low_precision(reason="mean|w| scale, 1e-4 rel is ample"):
                        i = v.reduce_sum(out=mcol1[:], in_=mpart[:, :ntile], axis=AXX)
                    S.inc(i, "dv")
                mkc = S.val("dv")

                @block.tensor
                def _(t, mkc=mkc):
                    S.wait(t, "dv", max(mkc, mk_consts))
                    i = t.matmul(out=ps_w[0:1, 0:1], lhsT=mcol1[:],
                                 rhs=ones_col[:].bitcast(F32),
                                 start=True, stop=True)
                    S.inc(i, "pe")
                mk_tot = S.val("pe")

                @block.vector
                def _(v, mk_tot=mk_tot):
                    S.wait(v, "pe", mk_tot)
                    with nc.allow_low_precision(reason="weight-quant scale, 1e-4 rel ample"):
                        i = v.tensor_scalar(out=sc11[:, 0:1], in0=ps_w[0:1, 0:1],
                                            scalar1=1.0 / (INNER * D), scalar2=1e-5,
                                            op0=ALU.mult, op1=ALU.max)
                        S.guard(v, i, "dv")
                        i = v.reciprocal(out=sc11[:, 1:2], in_=sc11[:, 0:1])
                    S.inc(i, "dv")
                mk_sc11 = S.val("dv")

                @block.tensor
                def _(t, mk_sc11=mk_sc11):
                    S.wait(t, "dv", mk_sc11)
                    t.matmul(out=ps_w[:, 1:2], lhsT=ones_row[:].bitcast(F32),
                             rhs=sc11[0:1, 0:1], start=True, stop=True)
                    i = t.matmul(out=ps_w[:, 2:3], lhsT=ones_row[:].bitcast(F32),
                                 rhs=sc11[0:1, 1:2], start=True, stop=True)
                    S.inc(i, "pe")
                mk_bc = S.val("pe")

                @block.vector
                def _(v, invcol=invcol, mk_bc=mk_bc):
                    S.wait(v, "pe", mk_bc)
                    v.tensor_copy(out=invcol[:], in_=ps_w[:, 1:2])
                    i = v.tensor_copy(out=swcol[:], in_=ps_w[:, 2:3])
                    S.inc(i, "dv")
                mk_sw = S.val("dv")

                q_marks = [mk_sw, mk_sw]
                st_marks_w = [st_tail[0], st_tail[1]]
                for it in range(ntile):
                    p = it % 2
                    buf = wtmp0 if p == 0 else wtmp1
                    qb = wqb0 if p == 0 else wqb1

                    @block.sync
                    def _(sy, it=it, apw=apw, tw=tw, buf=buf, p=p,
                          mk_free=q_marks[it]):
                        S.wait(sy, "dv", mk_free)
                        S.wait(sy, f"qa{p}", S.val(f"qa{p}"))
                        sy.dma_start(out=buf[:, :tw],
                                     in_=apw[it * 128:(it + 1) * 128, :]
                                     ).then_inc(S.h[f"qa{p}"], 16)
                    S.c[f"qa{p}"] += 16
                    mk = S.val(f"qa{p}")

                    @block.vector
                    def _(v, tw=tw, mk=mk, mk_sw=mk_sw, buf=buf, qb=qb, p=p,
                          mk_st=st_marks_w[it], mk_pq=q_marks[-1]):
                        S.wait(v, f"qa{p}", mk)
                        S.wait(v, "dv", max(mk_sw, mk_pq))
                        S.wait(v, f"qs{p}", mk_st)
                        i = v.tensor_scalar_mul(out=wi8[:, :tw], in0=buf[:, :tw],
                                                scalar1=swcol[:])
                        S.guard(v, i, "dv")
                        i = v.tensor_scalar(out=qb[:, :tw], in0=wi8[:, :tw],
                                            scalar1=1.0, scalar2=-1.0,
                                            op0=ALU.min, op1=ALU.max)
                        S.inc(i, "dv")
                    q_marks.append(S.val("dv"))
                    mkq = S.val("dv")

                    @block.gpsimd
                    def _(g, it=it, dscr=dscr, tw=tw, mkq=mkq, qb=qb, p=p):
                        S.wait(g, "dv", mkq)
                        S.wait(g, f"qs{p}", S.val(f"qs{p}"))
                        g.dma_start(out=dscr[it * 128:(it + 1) * 128, :],
                                    in_=qb[:, :tw]).then_inc(S.h[f"qs{p}"], 16)
                    S.c[f"qs{p}"] += 16
                    st_marks_w.append(S.val(f"qs{p}"))
                q_marks = q_marks[-2:]
                st_tail = {0: S.val("qs0"), 1: S.val("qs1")}
            mk_wq0 = S.val("qs0")
            mk_wq1 = S.val("qs1")

            esW.close()
        else:
            mk_wq0 = mk_wq1 = 0

        xin = sbB("xin", [128, 2 * D], F32)
        xnbf = sbB("xnbf", [128, 2 * D], BF16)
        xnT = sbB("xnT", [128, 2 * 1024], BF16)
        lncol = sbB("lncol", [128, 8], F32)
        lnst = sbB("lnst", [128, 12], F32)
        QTb = sbB("QTb", [128, 2 * 512], BF16)
        Kb = sbB("Kb", [128, 512], BF16)
        KTb = sbB("KTb", [128, 2 * 512], BF16)
        Vb = sbB("Vb", [128, 2 * 512], BF16)
        spf = sbB("spf", [128, 512], F32R)
        etmp = sbB("etmp", [128, 512], F32)
        stb = sbB("stb", [128, 2 * 512], BF16)
        Df = sbB("Df", [128, 2 * 512], F32)
        Dinv = sbB("Dinv", [128, 512], F32)
        dcrow = sbB("dcrow", [1, 512], F32R)
        dccol = sbB("dccol", [128, 4], F32)
        atm = sbB("atm", [128, 128], BF16)
        btm = sbB("btm", [128, 128], BF16)
        stT = sbB("stT", [128, 256], BF16)
        qtT = sbB("qtT", [128, 256], BF16)
        okf = sbB("okf", [128, 256], F32)
        eok = sbB("eok", [128, 256], F32)
        qtb = sbB("qtb", [128, 256], BF16)
        smcol = sbB("smcol", [128, 4], F32)
        s1f = sbB("s1f", [128, 1024], F32)
        s2f = sbB("s2f", [128, 1024], F32)
        s1b = sbB("s1b", [128, 3 * 1024], BF16)
        s2b = sbB("s2b", [128, 3 * 1024], BF16)
        tmpu = sbB("tmpu", [128, 1024], F32)
        sf = tmpu[:, 512:1024]
        ov_sb = sbB("ov_sb", [128, 4 * T], BF16)

        ps_q = pstB("ps_q", [128, 512])
        ps_k = pstB("ps_k", [128, 512])
        ps_v = pstB("ps_v", [128, 512])
        ps_f = pstB("ps_f", [128, 512])
        ps_g = pstB("ps_g", [128, 512])
        ps_m1 = pstB("ps_m1", [128, 512])
        ps_m2 = pstB("ps_m2", [128, 512])
        ps_i = pstB("ps_i", [128, 512])

        mk_xn_hist = {}
        mk_xnT_hist = {}
        mk_proj_hist = {}
        slot_xn = {0: 0, 1: 0}
        slot_xnT = {0: 0, 1: 0}
        slot_proj = {0: 0, 1: 0}
        dump_marks = []
        mk_state = 0
        for hpg in range(1 if mix_stop else 2):
            @block.gpsimd
            def _(g, hpg=hpg):
                S.wait(g, "pe", S.val("pe"))
                base = S.val("qw")
                for wi, (apw, dst) in enumerate(((ap_wqT, wq_b), (ap_wkT, wk_b),
                                                 (ap_wvT, wv_b), (ap_wfT, wf_b))):
                    g.wait_ge(S.h["qw"], base + wi * 16)
                    g.dma_start(out=dst[:].rearrange("p (b f) -> p b f", b=8),
                                in_=apw.rearrange("(b p) f -> p b f", p=128)
                                [:, :, hpg * 512:(hpg + 1) * 512]
                                ).then_inc(S.h["qw"], 16)
            S.c["qw"] += 64
            mk_wload_h = S.val("qw")

            @block.vector
            def _(v):
                S.wait(v, "pe", S.val("pe"))
                S.wait(v, "sc", S.val("sc"))
                if hpg == 0 and do_ffn:
                    S.wait(v, "qs0", mk_wq0)
                    S.wait(v, "qs1", mk_wq1)
                if dump_marks:
                    S.wait(v, "qst", dump_marks[-1])
                v.memset(s1f[:], 0.0)
                v.memset(s2f[:], 0.0)
                v.memset(s1b[:], 0.0)
                v.memset(ov_sb[:], 0.0)
                i = v.memset(s2b[:], 0.0)
                S.inc(i, "dv")
            mk_state0h = S.val("dv")

            M = {}

            def emit_LN(c):
                pp = c % 2
                gi = hpg * nch + c
                xslice = xin[:, pp * D:(pp + 1) * D]
                xnbs = xnbf[:, pp * D:(pp + 1) * D]

                prev_xn = slot_xn[pp]

                @block.gpsimd
                def _(g, c=c, pp=pp, prev_xn=prev_xn, gi=gi):
                    S.wait(g, "dv", prev_xn)
                    if gi == 0 and do_ffn:
                        S.wait(g, "qs0", mk_wq0)
                        S.wait(g, "qs1", mk_wq1)
                    S.wait(g, f"qx{pp}", S.val(f"qx{pp}"))
                    g.dma_start(out=xin[:, pp * D:(pp + 1) * D],
                                in_=ap_x[c * C:(c + 1) * C, :]
                                ).then_inc(S.h[f"qx{pp}"], 16)
                S.c[f"qx{pp}"] += 16
                mk_x = S.val(f"qx{pp}")

                @block.vector
                def _(v, mk_x=mk_x, xslice=xslice, pp=pp):
                    S.wait(v, f"qx{pp}", mk_x)
                    v.bn_stats(out=lnst[:, 0:6], in_=xslice[:, 0:512])
                    i = v.bn_stats(out=lnst[:, 6:12], in_=xslice[:, 512:1024])
                    S.guard(v, i, "dv")
                    i = v.bn_aggr(out=lncol[:, 0:2], in_=lnst[:, 0:12])
                    S.guard(v, i, "dv")
                    i = v.tensor_scalar_add(out=lncol[:, 2:3], in0=lncol[:, 1:2],
                                            scalar1=1e-5)
                    S.guard(v, i, "dv")
                    i = v.reciprocal(out=lncol[:, 6:7], in_=lncol[:, 2:3])
                    S.inc(i, "dv")
                mk_ln1 = S.val("dv")

                @block.scalar
                def _(sc, mk_ln1=mk_ln1):
                    S.wait(sc, "dv", mk_ln1)
                    i = sc.activation(out=lncol[:, 3:4], in_=lncol[:, 6:7], func=AF.Sqrt)
                    S.inc(i, "sc")
                mk_rv = S.val("sc")
                M.setdefault(c, {})["rv"] = mk_rv

                @block.vector
                def _(v, mk_rv=mk_rv, xslice=xslice, xnbs=xnbs, gi=gi, pp=pp):
                    S.wait(v, "sc", mk_rv)
                    S.wait(v, f"qt{pp}", slot_xnT[pp])
                    i = v.tensor_scalar_mul(out=lncol[:, 4:5], in0=lncol[:, 0:1],
                                            scalar1=-1.0)
                    S.guard(v, i, "dv")
                    i = v.tensor_scalar(out=xnbs, in0=xslice, scalar1=lncol[:, 4:5],
                                        scalar2=lncol[:, 3:4], op0=ALU.add, op1=ALU.mult)
                    S.inc(i, "dv")
                mk_xn_hist[gi] = S.val("dv")
                slot_xn[pp] = mk_xn_hist[gi]

                @block.sync
                def _(sy, pp=pp, gi=gi, xnbs=xnbs, prev_proj=slot_proj[pp]):
                    S.wait(sy, "dv", mk_xn_hist[gi])
                    S.wait(sy, "pe", prev_proj)
                    S.wait(sy, f"qt{pp}", S.val(f"qt{pp}"))
                    sy.dma_start(out=xnT[:, pp * 1024:(pp + 1) * 1024]
                                 .rearrange("p (b f) -> p b f", b=8),
                                 in_=xnbs.rearrange("p (b f) -> p b f", b=8),
                                 transpose=True).then_inc(S.h[f"qt{pp}"], 16)
                S.c[f"qt{pp}"] += 16
                mk_xnT_hist[gi] = S.val(f"qt{pp}")
                slot_xnT[pp] = mk_xnT_hist[gi]

            def emit_A2a(c):
                pp = c % 2
                gi = hpg * nch + c
                sl2 = pp * 512
                xnTs = lambda dc: xnT[:, pp * 1024 + dc * 128:pp * 1024 + (dc + 1) * 128]

                @block.tensor
                def _(t, pp=pp, c=c, gi=gi,
                      mk_prev=M.get(c - 1, {}).get("state", mk_state0h)):
                    S.wait(t, f"qt{pp}", mk_xnT_hist[gi])
                    S.wait(t, "dv", mk_prev)
                    if c == 0:
                        S.wait(t, "qw", mk_wload_h)
                        S.wait(t, "gp", mk_masks)
                        if dump_marks:
                            S.wait(t, "qst", dump_marks[-1])
                    for dst_ps, wsb in ((ps_f, wf_b), (ps_k, wk_b)):
                        for dc in range(8):
                            i = t.matmul(out=dst_ps[:], lhsT=xnTs(dc),
                                         rhs=wsb[:, dc * 512:(dc + 1) * 512],
                                         start=(dc == 0), stop=(dc == 7))
                        S.inc(i, "pe")
                mk_fk = S.val("pe")

                @block.scalar
                def _(sc, mk_fk=mk_fk):
                    S.wait(sc, "pe", mk_fk - 1)
                    i = sc.activation(out=etmp[:], in_=ps_f[:], func=AF.Exp, scale=-1.0)
                    S.guard(sc, i, "sc")
                    i = sc.activation(out=spf[:], in_=etmp[:], func=AF.Ln, bias=1.0)
                    S.inc(i, "sc")
                mk_spf = S.val("sc")

                @block.tensor
                def _(t, mk_spf=mk_spf, mk_pst=M.get(c - 1, {}).get("st", 0)):
                    S.wait(t, "sc", mk_spf)
                    S.wait(t, "dv", mk_pst)
                    t.matmul(out=ps_g[:], lhsT=utones[:], rhs=spf[:],
                             start=True, stop=True)
                    i = t.matmul(out=ps_m2[0:1, 0:512], lhsT=ones_col[:], rhs=spf[:],
                                 start=True, stop=True)
                    S.inc(i, "pe")
                mk_ktg = S.val("pe")

                @block.scalar
                def _(sc, mk_ktg=mk_ktg, sl2=sl2):
                    S.wait(sc, "pe", mk_ktg)
                    sc.activation(out=Df[:, sl2:sl2 + 512], in_=ps_g[:], func=AF.Exp,
                                  scale=-0.125)
                    sc.activation(out=Dinv[:], in_=ps_g[:], func=AF.Exp, scale=0.125)
                    i = sc.activation(out=dcrow[:], in_=ps_m2[0:1, 0:512], func=AF.Exp,
                                      scale=-0.125)
                    S.guard(sc, i, "sc")
                    i = sc.activation(out=etmp[:], in_=spf[:], func=AF.Exp, scale=-0.125)
                    S.inc(i, "sc")
                M[c]["dd"] = S.val("sc")
                mk_dd = M[c]["dd"]

                @block.tensor
                def _(t, pp=pp, xnTs=xnTs):
                    for j in range(4):
                        for dc in range(8):
                            i = t.matmul(out=ps_q[:, j * 128:(j + 1) * 128],
                                         lhsT=wq_b[:, dc * 512 + j * 128:dc * 512 + (j + 1) * 128],
                                         rhs=xnTs(dc), start=(dc == 0), stop=(dc == 7))
                    S.inc(i, "pe")
                    for dc in range(8):
                        i = t.matmul(out=ps_v[:], lhsT=xnTs(dc),
                                     rhs=wv_b[:, dc * 512:(dc + 1) * 512],
                                     start=(dc == 0), stop=(dc == 7))
                    S.inc(i, "pe")
                mk_proj_hist[gi] = S.val("pe")
                slot_proj[pp] = mk_proj_hist[gi]
                mk_proj = mk_proj_hist[gi]

                @block.tensor
                def _(t, mk_dd=mk_dd):
                    S.wait(t, "sc", mk_dd)
                    for ms in range(4):
                        i = t.matmul(out=ps_m2[:, ms * 128:(ms + 1) * 128],
                                     lhsT=dcrow[0:1, ms * 128:(ms + 1) * 128],
                                     rhs=ones_row[:], start=True, stop=True)
                    S.inc(i, "pe")
                    i = t.matmul(out=ps_g[:], lhsT=ones_row[:], rhs=dcrow[:],
                                 start=True, stop=True)
                    S.inc(i, "pe")
                mk_dc = S.val("pe")

                @block.scalar
                def _(sc, mk_proj=mk_proj, sl2=sl2):
                    S.wait(sc, "pe", mk_proj)
                    sc.activation(out=QTb[:, sl2:sl2 + 512], in_=ps_q[:], func=AF.Silu)
                    sc.activation(out=Kb[:], in_=ps_k[:], func=AF.Silu)
                    i = sc.activation(out=Vb[:, sl2:sl2 + 512], in_=ps_v[:], func=AF.Silu)
                    S.inc(i, "sc")
                M[c]["silu"] = S.val("sc")
                mk_silu = M[c]["silu"]

                @block.vector
                def _(v, mk_dc=mk_dc, mk_dd=mk_dd, sl2=sl2,
                      mk_pstate=M.get(c - 1, {}).get("state", 0)):
                    S.wait(v, "sc", mk_dd)
                    S.wait(v, "pe", mk_dc)
                    S.wait(v, "dv", mk_pstate)
                    v.tensor_copy(out=dccol[:], in_=ps_m2[:, 0:512:128])
                    i = v.tensor_scalar(out=sf, in0=etmp[:], scalar1=-1.0, scalar2=1.0,
                                        op0=ALU.mult, op1=ALU.add)
                    S.guard(v, i, "dv")
                    i = v.tensor_tensor(out=stb[:, sl2:sl2 + 512], in0=sf, in1=Dinv[:],
                                        op=ALU.mult)
                    S.inc(i, "dv")
                M[c]["st"] = S.val("dv")

                @block.tensor
                def _(t, mk_silu=mk_silu,
                      mk_chg=M.get(c - 1, {}).get("atm0", 0)):
                    S.wait(t, "sc", mk_silu)
                    S.wait(t, "dv", mk_chg)
                    for j in range(4):
                        i = t.matmul(out=ps_m1[:, j * 128:(j + 1) * 128],
                                     lhsT=Kb[:, j * 128:(j + 1) * 128], rhs=identb[:],
                                     start=True, stop=True)
                    S.inc(i, "pe")
                mk_ktm = S.val("pe")

                @block.vector
                def _(v, mk_ktm=mk_ktm, sl2=sl2):
                    S.wait(v, "pe", mk_ktm)
                    i = v.tensor_copy(out=KTb[:, sl2:sl2 + 512], in_=ps_m1[:])
                    S.inc(i, "dv")
                M[c]["ktb"] = S.val("dv")

            def emit_B1_mm(c):
                sl2 = (c % 2) * 512
                mk_st = M[c]["st"]

                @block.tensor
                def _(t, mk_st=mk_st, sl2=sl2):
                    S.wait(t, "dv", mk_st)
                    for h, (pu1, pu2) in enumerate(((ps_q, ps_k), (ps_v, ps_f))):
                        hs = sl2 + h * 256
                        for dkh in range(2):
                            t.matmul(out=pu1[:, dkh * 256:(dkh + 1) * 256],
                                     lhsT=Kb[:, h * 256 + dkh * 128:h * 256 + (dkh + 1) * 128],
                                     rhs=stb[:, hs:hs + 256], start=True, stop=True)
                        for mh in range(2):
                            i = t.matmul(out=pu2[:, mh * 256:(mh + 1) * 256],
                                         lhsT=stb[:, hs + mh * 128:hs + (mh + 1) * 128],
                                         rhs=Vb[:, hs:hs + 256], start=True, stop=True)
                        S.inc(i, "pe")
                M[c]["u"] = S.val("pe")

            def emit_B1_vec(c):
                wsl = (c % 3) * 1024
                mk_u = M[c]["u"]

                @block.vector
                def _(v, mk_u=mk_u, wsl=wsl):
                    for h, (pu1, pu2) in enumerate(((ps_q, ps_k), (ps_v, ps_f))):
                        S.wait(v, "pe", mk_u - 1 + h)
                        for dkh in range(2):
                            sl = slice(h * 512 + dkh * 256, h * 512 + (dkh + 1) * 256)
                            v.tensor_tensor(out=tmpu[:, dkh * 256:(dkh + 1) * 256],
                                            in0=pu1[:, dkh * 256:(dkh + 1) * 256],
                                            in1=s1f[:, sl], op=ALU.add)
                        for mh in range(2):
                            sl = slice(h * 512 + mh * 256, h * 512 + (mh + 1) * 256)
                            i = v.tensor_tensor(out=tmpu[:, 512 + mh * 256:768 + mh * 256],
                                                in0=pu2[:, mh * 256:(mh + 1) * 256],
                                                in1=s2f[:, sl], op=ALU.add)
                        S.guard(v, i, "dv")
                        for dkh in range(2):
                            sl = slice(h * 512 + dkh * 256, h * 512 + (dkh + 1) * 256)
                            slw = slice(wsl + h * 512 + dkh * 256,
                                        wsl + h * 512 + (dkh + 1) * 256)
                            ts = tmpu[:, dkh * 256:(dkh + 1) * 256]
                            v.tensor_tensor(out=s1f[:, sl], in0=ts,
                                            in1=ps_g[:, h * 256:(h + 1) * 256],
                                            op=ALU.mult)
                            v.tensor_tensor(out=s1b[:, slw], in0=ts,
                                            in1=ps_g[:, h * 256:(h + 1) * 256],
                                            op=ALU.mult)
                        for mh in range(2):
                            sl = slice(h * 512 + mh * 256, h * 512 + (mh + 1) * 256)
                            slw = slice(wsl + h * 512 + mh * 256,
                                        wsl + h * 512 + (mh + 1) * 256)
                            ts = tmpu[:, 512 + mh * 256:768 + mh * 256]
                            v.tensor_scalar_mul(out=s2f[:, sl], in0=ts,
                                                scalar1=dccol[:, h * 2 + mh:h * 2 + mh + 1])
                            i = v.tensor_scalar_mul(out=s2b[:, slw], in0=ts,
                                                    scalar1=dccol[:, h * 2 + mh:h * 2 + mh + 1])
                        if h == 0:
                            S.guard(v, i, "dv")
                        else:
                            S.inc(i, "dv")
                M[c]["state"] = S.val("dv")

            def chainA(c, h, mk_extra=0):
                sl2 = (c % 2) * 512
                hs = sl2 + h * 256

                @block.tensor
                def _(t, hs=hs, mk_w=max(M[c]["st"], mk_extra)):
                    S.wait(t, "dv", mk_w)
                    for dkh in range(2):
                        i = t.matmul(out=ps_m1[:, 0:128],
                                     lhsT=KTb[:, hs + dkh * 128:hs + (dkh + 1) * 128],
                                     rhs=QTb[:, hs + dkh * 128:hs + (dkh + 1) * 128],
                                     start=(dkh == 0), stop=(dkh == 1))
                    for mh in range(2):
                        i = t.matmul(out=ps_m1[:, 128 + mh * 128:128 + (mh + 1) * 128],
                                     lhsT=stb[:, hs + mh * 128:hs + (mh + 1) * 128],
                                     rhs=identb[:], start=True, stop=True)
                    S.inc(i, "pe")
                mk_at = S.val("pe")

                @block.vector
                def _(v, mk_at=mk_at):
                    S.wait(v, "pe", mk_at)
                    v.tensor_tensor(out=atm[:], in0=ps_m1[:, 0:128], in1=utonesb[:],
                                    op=ALU.mult)
                    i = v.tensor_copy(out=stT[:], in_=ps_m1[:, 128:384])
                    S.inc(i, "dv")
                M[c][f"atm{h}"] = S.val("dv")

            def chainB(c, h):
                sl2 = (c % 2) * 512
                hs = sl2 + h * 256
                rsl = ((c - 1) % 3) * 1024

                @block.tensor
                def _(t, h=h, hs=hs, mk_atm=M[c][f"atm{h}"], rsl=rsl):
                    S.wait(t, "dv", mk_atm)
                    t.matmul(out=ps_i[:, 0:256], lhsT=atm[:], rhs=stb[:, hs:hs + 256],
                             start=True, stop=False)
                    for dkh in range(2):
                        i = t.matmul(out=ps_i[:, 0:256],
                                     lhsT=QTb[:, hs + dkh * 128:hs + (dkh + 1) * 128],
                                     rhs=s1b[:, rsl + h * 512 + dkh * 256:
                                              rsl + h * 512 + (dkh + 1) * 256],
                                     start=False, stop=(dkh == 1))
                    S.inc(i, "pe")
                mk_i1 = S.val("pe")

                @block.vector
                def _(v, hs=hs, mk_i1=mk_i1):
                    S.wait(v, "pe", mk_i1)
                    i = v.tensor_tensor(out=okf[:], in0=ps_i[:, 0:256],
                                        in1=Df[:, hs:hs + 256], op=ALU.mult)
                    S.guard(v, i, "dv")
                    i = v.reduce_max(out=smcol[:, 0:1], in_=okf[:], axis=AXX,
                                     negate=True)
                    S.inc(i, "dv")
                mk_ok = S.val("dv")

                if c == 0 and h == 0 and f"d_g{hpg}_ok" in dbg_aps:
                    @block.gpsimd
                    def _(g, mk_ok=mk_ok, hpg=hpg):
                        S.wait(g, "dv", mk_ok)
                        g.dma_start(out=dbg_aps[f"d_g{hpg}_ok"], in_=okf[:]).then_inc(S.h["qst"], 16)
                        g.dma_start(out=dbg_aps[f"d_g{hpg}_atm"], in_=atm[:]).then_inc(S.h["qst"], 16)
                        g.dma_start(out=dbg_aps[f"d_g{hpg}_s1b"], in_=s1b[:, 0:1024]).then_inc(S.h["qst"], 16)
                    S.c["qst"] += 48

                @block.scalar
                def _(sc, mk_ok=mk_ok, mk_scg=S.val("sc")):
                    S.wait(sc, "dv", mk_ok)
                    S.wait(sc, "sc", mk_scg)
                    i = sc.activation(out=eok[:], in_=okf[:], func=AF.Exp,
                                      bias=smcol[:, 0:1], accum_out=smcol[:, 1:2])
                    S.inc(i, "sc")
                mk_eok = S.val("sc")

                @block.vector
                def _(v, hs=hs, mk_eok=mk_eok):
                    S.wait(v, "sc", mk_eok)
                    i = v.reciprocal(out=smcol[:, 2:3], in_=smcol[:, 1:2])
                    S.guard(v, i, "dv")
                    i = v.scalar_tensor_tensor(out=qtb[:], in0=eok[:],
                                               scalar=smcol[:, 2:3],
                                               in1=Df[:, hs:hs + 256],
                                               op0=ALU.mult, op1=ALU.mult)
                    S.inc(i, "dv")
                M[c][f"qt{h}"] = S.val("dv")

            def chainC(c, h, mk_stcur):
                @block.tensor
                def _(t, mk_w=max(M[c][f"qt{h}"], mk_stcur)):
                    S.wait(t, "dv", mk_w)
                    for mh in range(2):
                        i = t.matmul(out=ps_m1[:, 128 + mh * 128:128 + (mh + 1) * 128],
                                     lhsT=qtb[:, mh * 128:(mh + 1) * 128], rhs=identb[:],
                                     start=True, stop=True)
                    S.inc(i, "pe")
                mk_qtTp = S.val("pe")

                @block.vector
                def _(v, mk_qtTp=mk_qtTp):
                    S.wait(v, "pe", mk_qtTp)
                    i = v.tensor_copy(out=qtT[:], in_=ps_m1[:, 128:384])
                    S.inc(i, "dv")
                M[c][f"qtTv{h}"] = S.val("dv")

            def chainD(c, h, mk_stcur):
                @block.tensor
                def _(t, mk_w=max(M[c][f"qtTv{h}"], mk_stcur)):
                    S.wait(t, "dv", mk_w)
                    for mh in range(2):
                        i = t.matmul(out=ps_m1[:, 384:512],
                                     lhsT=stT[:, mh * 128:(mh + 1) * 128],
                                     rhs=qtT[:, mh * 128:(mh + 1) * 128],
                                     start=(mh == 0), stop=(mh == 1))
                    S.inc(i, "pe")
                mk_bt = S.val("pe")

                @block.vector
                def _(v, mk_bt=mk_bt):
                    S.wait(v, "pe", mk_bt)
                    i = v.tensor_tensor(out=btm[:], in0=ps_m1[:, 384:512], in1=utonesb[:],
                                        op=ALU.mult)
                    S.inc(i, "dv")
                M[c][f"btm{h}"] = S.val("dv")

            def chainE(c, h):
                sl2 = (c % 2) * 512
                hs = sl2 + h * 256
                rsl = ((c - 1) % 3) * 1024

                @block.tensor
                def _(t, h=h, hs=hs, mk_btm=M[c][f"btm{h}"], rsl=rsl):
                    S.wait(t, "dv", mk_btm)
                    for vh in range(2):
                        t.matmul(out=ps_i[:, 256 + vh * 128:256 + (vh + 1) * 128],
                                 lhsT=Vb[:, hs + vh * 128:hs + (vh + 1) * 128],
                                 rhs=btm[:], start=True, stop=False)
                        for mh in range(2):
                            i = t.matmul(out=ps_i[:, 256 + vh * 128:256 + (vh + 1) * 128],
                                         lhsT=s2b[:, rsl + h * 512 + mh * 256 + vh * 128:
                                                  rsl + h * 512 + mh * 256 + (vh + 1) * 128],
                                         rhs=qtT[:, mh * 128:(mh + 1) * 128],
                                         start=False, stop=(mh == 1))
                    S.inc(i, "pe")
                mk_ovp = S.val("pe")

                @block.vector
                def _(v, mk_ovp=mk_ovp):
                    S.wait(v, "pe", mk_ovp)
                    i = v.tensor_copy(out=eok[:], in_=ps_i[:, 256:512])
                    S.inc(i, "dv")
                mk_ove = S.val("dv")

                @block.scalar
                def _(sc, h=h, c=c, mk_ove=mk_ove):
                    S.wait(sc, "dv", mk_ove)
                    for vh in range(2):
                        i = sc.activation(
                            out=ov_sb[:, (2 * h + vh) * T + c * 128:
                                      (2 * h + vh) * T + (c + 1) * 128],
                            in_=eok[:, vh * 128:(vh + 1) * 128],
                            func=AF.Silu)
                    S.inc(i, "sc")

            emit_LN(0)
            for c in range(nch):
                if c + 1 < nch:
                    emit_LN(c + 1)
                if c >= 1:
                    chainA(c - 1, 0, M[c - 1]["ktb"])
                emit_A2a(c)
                if c >= 1:
                    chainB(c - 1, 0)
                if c == 0:
                    pref = f"d_g{hpg}_"
                    if pref + "ln" in dbg_aps:
                        @block.gpsimd
                        def _(g, pref=pref, mk_rv=M[0]["rv"]):
                            S.wait(g, "sc", mk_rv)
                            g.dma_start(out=dbg_aps[pref + "ln"], in_=lncol[:]).then_inc(S.h["qst"], 16)
                            g.dma_start(out=dbg_aps[pref + "ls"], in_=lnst[:]).then_inc(S.h["qst"], 16)
                        S.c["qst"] += 32
                    if pref + "st" in dbg_aps:
                        @block.gpsimd
                        def _(g, pref=pref, mk_st=M[0]["st"], mkq=S.val("qt0")):
                            S.wait(g, "dv", mk_st)
                            S.wait(g, "qt0", mkq)
                            g.dma_start(out=dbg_aps[pref + "xi"], in_=xin[:, 0:D]).then_inc(S.h["qst"], 16)
                            g.dma_start(out=dbg_aps[pref + "xn"], in_=xnbf[:, 0:D]).then_inc(S.h["qst"], 16)
                            g.dma_start(out=dbg_aps[pref + "xt"], in_=xnT[:, 0:1024]).then_inc(S.h["qst"], 16)
                            g.dma_start(out=dbg_aps[pref + "st"], in_=stb[:, 0:512]).then_inc(S.h["qst"], 16)
                            g.dma_start(out=dbg_aps[pref + "sp"], in_=spf[:]).then_inc(S.h["qst"], 16)
                            g.dma_start(out=dbg_aps[pref + "df"], in_=Df[:, 0:512]).then_inc(S.h["qst"], 16)
                            g.dma_start(out=dbg_aps[pref + "di"], in_=Dinv[:]).then_inc(S.h["qst"], 16)
                            g.dma_start(out=dbg_aps[pref + "kb"], in_=Kb[:]).then_inc(S.h["qst"], 16)
                            g.dma_start(out=dbg_aps[pref + "qt"], in_=QTb[:, 0:512]).then_inc(S.h["qst"], 16)
                        S.c["qst"] += 144
                if mix_stop and c == 0:
                    break
                if c >= 1:
                    stc = M[c]["ktb"]
                    chainC(c - 1, 0, stc)
                    chainD(c - 1, 0, stc)
                emit_B1_mm(c)
                if c >= 1:
                    chainE(c - 1, 0)
                    chainA(c - 1, 1, M[c - 1]["qtTv0"])
                    chainB(c - 1, 1)
                    chainC(c - 1, 1, stc)
                    chainD(c - 1, 1, stc)
                    chainE(c - 1, 1)
                emit_B1_vec(c)
            if mix_stop:
                mk_state = mk_state0h
            else:
                cl = nch - 1
                stc = M[cl]["ktb"]
                chainA(cl, 0, M[cl]["ktb"])
                chainB(cl, 0)
                chainC(cl, 0, stc)
                chainD(cl, 0, stc)
                chainE(cl, 0)
                chainA(cl, 1, M[cl]["qtTv0"])
                chainB(cl, 1)
                chainC(cl, 1, stc)
                chainD(cl, 1, stc)
                chainE(cl, 1)
                mk_state = M[cl]["state"]

            if mix_stop:
                continue
            @block.gpsimd
            def _(g, hpg=hpg):
                S.wait(g, "dv", S.val("dv"))
                S.wait(g, "sc", S.val("sc"))
                S.wait(g, "qst", S.val("qst"))
                g.dma_start(out=d_ccout.ap()[hpg * 512:(hpg + 1) * 512, :]
                            .rearrange("(g p) t -> p g t", p=128),
                            in_=ov_sb[:].rearrange("p (g t) -> p g t", g=4)
                            ).then_inc(S.h["qst"], 16)
            S.c["qst"] += 16
            dump_marks.append(S.val("qst"))


        if mix_stop:
            @block.gpsimd
            def _(g):
                S.wait(g, "qst", S.val("qst"))
                g.engine_nop()
            esB.close()
            return nc
        mk_mixer = mk_state
        mk_ccin = dump_marks[-1]

        if "d_cc" in dbg_aps:
            @block.gpsimd
            def _(g):
                S.wait(g, "qst", mk_ccin)
                g.dma_start(out=dbg_aps["d_cc"], in_=d_ccout.ap()[:, :]
                            ).then_inc(S.h["qst"], 16)
            S.c["qst"] += 16
        if "d_s1" in dbg_aps:
            @block.gpsimd
            def _(g):
                S.wait(g, "dv", mk_mixer)
                g.dma_start(out=dbg_aps["d_s1"], in_=s1f[:]).then_inc(S.h["qst"], 16)
                g.dma_start(out=dbg_aps["d_s2"], in_=s2f[:]).then_inc(S.h["qst"], 16)
            S.c["qst"] += 32

        if not do_ffn:
            @block.gpsimd
            def _(g):
                S.wait(g, "qst", S.val("qst"))
                g.engine_nop()
            esB.close()
            return nc

        esB.close()

        wslab = sb("wslab", [128, 4 * 4096], BF16)
        wdslab = sb("wdslab", [128, 2 * 1024], BF16)
        xqi8 = sb("xqi8", [128, D], I8)
        lncol = sb("lncol2", [128, 8], F32)
        lnst2 = sb("lnst2", [128, 12], F32)
        xqsb = sb("xqsb", [128, D], BF16)
        xqTg = sb("xqTg", [128, 8 * GTOK], BF16)
        xrb = sb("xrb", [128, 2 * D], F32)
        qcol = sb("qcol", [128, 16], F32)
        guT = sb("guT", [128, 32 * GTOK], F32)
        guqT = sb("guqT", [128, 32 * GTOK], I8)
        gcvt = sb("gcvt", [128, 2 * 256], BF16)
        sgT = sb("sgT", [128, 2 * GTOK], F32)
        sabs = sb("sabs", [128, GTOK], F32)
        mxM = [sb("mxM0", [128, GTOK], F32), sb("mxM1", [128, GTOK], F32)]
        mxT = sabs
        colmax = sb("colmax", [128, 4], F32)
        fcol = sb("fcol", [128, 16], F32)
        qrow = sb("qrow", [1, GTOK], F32R)
        ffn_g = sb("ffn_g", [128, 4 * D], F32)
        xrr = sb("xrr", [128, 2 * D], F32)
        ccA = sb("ccA", [128, 8 * 128], BF16)
        ccB = sb("ccB", [128, 8 * 128], BF16)
        osel = sb("osel", [128, 8 * 128], BF16)
        o2f = sb("o2f", [128, 8 * 128], F32R)
        rmsT = sb("rmsT", [128, 8 * 128], BF16)
        rrow = sb("rrow", [1, 256], F32R)
        outbuf = sb("outbuf", [128, D], F32)

        pd = [pst(f"pd{i}", [128, 1024]) for i in range(4)]

        st_marks = []
        xrr_marks = []
        def emit_xq(grp):
                xq_marks = []
                for bi in range(4):
                    blk = grp * 4 + bi
                    xpp = bi % 2

                    @block.gpsimd
                    def _(sy, blk=blk, bi=bi, grp=grp, xpp=xpp):
                        if grp == 0 and bi == 0:
                            S.wait(sy, "qst", mk_ccin)
                        if len(xq_marks) >= 2:
                            S.wait(sy, "qt0", xq_marks[-2])
                        S.wait(sy, f"qx{xpp}", S.val(f"qx{xpp}"))
                        sy.dma_start(out=xrb[:, xpp * D:(xpp + 1) * D],
                                     in_=ap_xr[blk * 128:(blk + 1) * 128, :]
                                     ).then_inc(S.h[f"qx{xpp}"], 16)
                    S.c[f"qx{xpp}"] += 16
                    mk_xr = S.val(f"qx{xpp}")

                    @block.vector
                    def _(v, bi=bi, mk_xr=mk_xr, xpp=xpp, grp=grp):
                        S.wait(v, f"qx{xpp}", mk_xr)
                        if grp == 0 and bi == 0:
                            S.wait(v, "qst", mk_ccin)
                        if xq_marks:
                            S.wait(v, "qt0", xq_marks[-1])
                        xb = xrb[:, xpp * D:(xpp + 1) * D]
                        v.bn_stats(out=lnst2[:, 0:6], in_=xb[:, 0:512])
                        i = v.bn_stats(out=lnst2[:, 6:12], in_=xb[:, 512:1024])
                        S.inc(i, "dv")
                        S.wait(v, "dv", S.val("dv"))
                        i = v.bn_aggr(out=lncol[:, 0:2], in_=lnst2[:, 0:12])
                        S.inc(i, "dv")
                        S.wait(v, "dv", S.val("dv"))
                        i = v.tensor_scalar_add(out=lncol[:, 2:3], in0=lncol[:, 1:2],
                                                  scalar1=1e-5)
                        S.guard(v, i, "dv")
                        v.reciprocal(out=lncol[:, 4:5], in_=lncol[:, 2:3])
                        v.tensor_reduce(out=lncol[:, 5:6], in_=xb, axis=AXX, op=ALU.max)
                        i = v.tensor_reduce(out=lncol[:, 6:7], in_=xb, axis=AXX, op=ALU.min)
                        S.inc(i, "dv")
                    mk_l = S.val("dv")

                    @block.scalar
                    def _(sc, mk_l=mk_l):
                        S.wait(sc, "dv", mk_l)
                        i = sc.activation(out=lncol[:, 3:4], in_=lncol[:, 4:5], func=AF.Sqrt)
                        S.inc(i, "sc")
                    mk_rv2 = S.val("sc")

                    @block.vector
                    def _(v, bi=bi, mk_rv2=mk_rv2, xpp=xpp):
                        S.wait(v, "sc", mk_rv2)
                        xb = xrb[:, xpp * D:(xpp + 1) * D]
                        g_ = lambda i: S.guard(v, i, "dv")
                        g_(v.tensor_tensor(out=lncol[:, 7:8], in0=lncol[:, 5:6],
                                           in1=lncol[:, 0:1], op=ALU.subtract))
                        g_(v.tensor_tensor(out=lncol[:, 4:5], in0=lncol[:, 0:1],
                                           in1=lncol[:, 6:7], op=ALU.subtract))
                        g_(v.tensor_tensor(out=lncol[:, 5:6], in0=lncol[:, 7:8],
                                           in1=lncol[:, 4:5], op=ALU.max))
                        g_(v.tensor_tensor(out=lncol[:, 6:7], in0=lncol[:, 5:6],
                                           in1=lncol[:, 3:4], op=ALU.mult))
                        g_(v.tensor_scalar_max(out=lncol[:, 5:6], in0=lncol[:, 6:7],
                                               scalar1=1e-5))
                        g_(v.reciprocal(out=lncol[:, 7:8], in_=lncol[:, 5:6]))
                        g_(v.tensor_scalar_mul(out=qcol[:, bi * 4 + 2:bi * 4 + 3],
                                               in0=lncol[:, 5:6], scalar1=1.0 / 127.0))
                        g_(v.scalar_tensor_tensor(out=qcol[:, bi * 4:bi * 4 + 1],
                                                  in0=lncol[:, 3:4], scalar=127.0,
                                                  in1=lncol[:, 7:8], op0=ALU.mult,
                                                  op1=ALU.mult))
                        g_(v.tensor_tensor(out=lncol[:, 6:7], in0=lncol[:, 0:1],
                                           in1=qcol[:, bi * 4:bi * 4 + 1], op=ALU.mult))
                        g_(v.tensor_scalar_mul(out=qcol[:, bi * 4 + 1:bi * 4 + 2],
                                               in0=lncol[:, 6:7], scalar1=-1.0))
                        g_(v.tensor_scalar(out=xqi8[:], in0=xb,
                                           scalar1=qcol[:, bi * 4:bi * 4 + 1],
                                           scalar2=qcol[:, bi * 4 + 1:bi * 4 + 2],
                                           op0=ALU.mult, op1=ALU.add))
                        i = v.tensor_scalar_mul(out=xqsb[:], in0=xqi8[:],
                                                scalar1=qcol[:, bi * 4 + 2:bi * 4 + 3])
                        S.inc(i, "dv")
                    mk_xq = S.val("dv")

                    if grp == 0 and bi == 0 and "d_xq" in dbg_aps:
                        @block.gpsimd
                        def _(g, mk_xq=mk_xq):
                            S.wait(g, "dv", mk_xq)
                            g.dma_start(out=dbg_aps["d_xq"], in_=xqsb[:]
                                        ).then_inc(S.h["qst"], 16)
                        S.c["qst"] += 16

                    @block.sync
                    def _(sy, bi=bi, mk_xq=mk_xq):
                        S.wait(sy, "dv", mk_xq)
                        S.wait(sy, "qt0", S.val("qt0"))
                        sy.dma_start(
                            out=xqTg[:].rearrange("p (b f) -> p b f", f=GTOK)
                            [:, :, bi * 128:(bi + 1) * 128],
                            in_=xqsb[:].rearrange("p (b f) -> p b f", b=8),
                            transpose=True).then_inc(S.h["qt0"], 16)
                    S.c["qt0"] += 16
                    xq_marks.append(S.val("qt0"))
                mk_xqT = S.val("qt0")
                return mk_xqT

        for grp in range(NGRP):
            mk_xqT = emit_xq(grp) if grp == 0 else pending_xqT

            @block.vector
            def _(v):
                i = v.memset(mxM[0][:], 0.0)
                S.inc(i, "dv")
            mk_mx0 = S.val("dv")

            slab_marks = []
            prod_marks = []
            absmax_marks = []
            for ic in range(8):
                pp = ic % 2

                @block.gpsimd
                def _(g, ic=ic, pp=pp, grp=grp):
                    if len(slab_marks) >= 2:
                        S.wait(g, "pe", slab_marks[-2])
                    elif grp > 0:
                        S.wait(g, "pe", grp_slab_tail[pp])
                    base = S.val(f"qg{pp}")
                    for widx, dscr in ((0, d_wgq), (1, d_wuq)):
                        if widx == 0 and grp == 0 and ic == 0:
                            S.wait(g, "qst", mk_ccin)
                            S.wait(g, "qs0", mk_wq0)
                            S.wait(g, "qs1", mk_wq1)
                        g.wait_ge(S.h[f"qg{pp}"], base + widx * 16)
                        g.dma_start(
                            out=wslab[:, (pp * 2 + widx) * 4096:(pp * 2 + widx + 1) * 4096]
                            .rearrange("p (b f) -> p b f", b=8),
                            in_=dscr.rearrange("(b p) i -> p b i", p=128)
                            [:, :, ic * 512:(ic + 1) * 512]).then_inc(S.h[f"qg{pp}"], 16)
                S.c[f"qg{pp}"] += 32
                mk_slab = S.val(f"qg{pp}")

                for isl in range(4):
                    sl_id = ic * 4 + isl

                    pdg = pd[sl_id % 2]
                    sgs = (sl_id % 2) * GTOK

                    @block.tensor
                    def _(t, pp=pp, isl=isl, mk_slab=mk_slab, mk_xqT=mk_xqT,
                          sl_id=sl_id, pdg=pdg):
                        S.wait(t, f"qg{pp}", mk_slab)
                        S.wait(t, "qt0", mk_xqT)
                        if len(prod_marks) >= 2:
                            S.wait(t, "dv", prod_marks[-2])
                        gbase = (pp * 2 + 0) * 4096
                        ubase = (pp * 2 + 1) * 4096
                        for dc in range(8):
                            i = t.matmul(out=pdg[:, 0:512],
                                         lhsT=wslab[:, gbase + dc * 512 + isl * 128:
                                                    gbase + dc * 512 + (isl + 1) * 128],
                                         rhs=xqTg[:, dc * GTOK:(dc + 1) * GTOK],
                                         start=(dc == 0), stop=(dc == 7))
                        for dc in range(8):
                            i = t.matmul(out=pdg[:, 512:1024],
                                         lhsT=wslab[:, ubase + dc * 512 + isl * 128:
                                                    ubase + dc * 512 + (isl + 1) * 128],
                                         rhs=xqTg[:, dc * GTOK:(dc + 1) * GTOK],
                                         start=(dc == 0), stop=(dc == 7))
                        S.inc(i, "pe")
                    slab_marks.append(S.val("pe"))
                    mk_gu = S.val("pe")

                    @block.scalar
                    def _(sc, mk_gu=mk_gu, pdg=pdg, sgs=sgs):
                        S.wait(sc, "pe", mk_gu)
                        i = sc.activation(out=sgT[:, sgs:sgs + GTOK], in_=pdg[:, 0:512],
                                          func=AF.Silu, scale=inv_swg[:])
                        S.inc(i, "sc")
                    mk_sg = S.val("sc")

                    if grp == 0 and sl_id == 0 and "d_sg" in dbg_aps:
                        @block.gpsimd
                        def _(g, mk_sg=mk_sg):
                            S.wait(g, "sc", mk_sg)
                            g.dma_start(out=dbg_aps["d_sg"], in_=sgT[:, 0:GTOK]
                                        ).then_inc(S.h["qst"], 16)
                        S.c["qst"] += 16

                    @block.vector
                    def _(v, sl_id=sl_id, mk_sg=mk_sg, pdg=pdg, sgs=sgs):
                        S.wait(v, "sc", mk_sg)
                        if grp == 0 and sl_id == 0 and "d_sg" in dbg_aps:
                            S.wait(v, "qst", S.val("qst"))
                        i = v.tensor_tensor(out=guT[:, sl_id * GTOK:(sl_id + 1) * GTOK],
                                            in0=pdg[:, 512:1024], in1=sgT[:, sgs:sgs + GTOK],
                                            op=ALU.mult)
                        S.inc(i, "dv")
                    prod_marks.append(S.val("dv"))
                    if grp == 0 and sl_id == 0 and "d_gu" in dbg_aps:
                        @block.gpsimd
                        def _(g, m=prod_marks[-1]):
                            S.wait(g, "dv", m)
                            g.dma_start(out=dbg_aps["d_gu"], in_=guT[:, 0:GTOK]
                                        ).then_inc(S.h["qst"], 16)
                        S.c["qst"] += 16
                    mk_prod1 = prod_marks[-1]

                    @block.vector
                    def _(v, sl_id=sl_id, mk_prod1=mk_prod1, mk_mx0=mk_mx0):
                        S.wait(v, "dv", max(mk_prod1, mk_mx0))
                        a, b = sl_id % 2, 1 - sl_id % 2
                        gus = guT[:, sl_id * GTOK:(sl_id + 1) * GTOK]
                        i = v.tensor_tensor(out=sabs[:], in0=gus, in1=mxM[a][:],
                                            op=ALU.max)
                        S.guard(v, i, "dv")
                        i = v.scalar_tensor_tensor(out=mxM[b][:], in0=gus, scalar=-1.0,
                                                   in1=sabs[:], op0=ALU.mult, op1=ALU.max)
                        S.inc(i, "dv")
                    absmax_marks.append(S.val("dv"))
            grp_slab_tail = {0: slab_marks[-2], 1: slab_marks[-1]}
            mk_absmax = absmax_marks[-1]

            @block.tensor
            def _(t, mk_absmax=mk_absmax):
                S.wait(t, "dv", mk_absmax)
                for bi in range(4):
                    i = t.matmul(out=pd[1][:, bi * 128:(bi + 1) * 128],
                                 lhsT=mxM[0][:, bi * 128:(bi + 1) * 128],
                                 rhs=identf[:], start=True, stop=True)
                S.inc(i, "pe")
            mk_mxT = S.val("pe")

            @block.vector
            def _(v, mk_mxT=mk_mxT):
                S.wait(v, "pe", mk_mxT)
                i = v.tensor_copy(out=mxT[:], in_=pd[1][:, 0:512])
                S.guard(v, i, "dv")
                for bi in range(4):
                    i = v.tensor_reduce(out=colmax[:, bi:bi + 1],
                                        in_=mxT[:, bi * 128:(bi + 1) * 128], axis=AXX,
                                        op=ALU.max)
                S.guard(v, i, "dv")
                g_ = lambda i: S.guard(v, i, "dv")
                for bi in range(4):
                    g_(v.tensor_scalar_mul(out=fcol[:, bi * 4 + 3:bi * 4 + 4],
                                           in0=colmax[:, bi:bi + 1], scalar1=inv_swu[:]))
                    g_(v.tensor_scalar(out=fcol[:, bi * 4:bi * 4 + 1],
                                       in0=fcol[:, bi * 4 + 3:bi * 4 + 4],
                                       scalar1=1e-5, scalar2=None, op0=ALU.max))
                    g_(v.reciprocal(out=fcol[:, bi * 4 + 1:bi * 4 + 2],
                                    in_=fcol[:, bi * 4:bi * 4 + 1]))
                    g_(v.scalar_tensor_tensor(out=fcol[:, bi * 4 + 2:bi * 4 + 3],
                                              in0=fcol[:, bi * 4 + 1:bi * 4 + 2],
                                              scalar=127.0, in1=inv_swu[:],
                                              op0=ALU.mult, op1=ALU.mult))
                    i = v.scalar_tensor_tensor(out=fcol[:, bi * 4 + 3:bi * 4 + 4],
                                               in0=fcol[:, bi * 4:bi * 4 + 1],
                                               scalar=1.0 / 127.0, in1=inv_swd[:],
                                               op0=ALU.mult, op1=ALU.mult)
                    if bi < 3:
                        g_(i)
                S.inc(i, "dv")
            mk_fcol = S.val("dv")

            @block.tensor
            def _(t, mk_fcol=mk_fcol):
                S.wait(t, "dv", mk_fcol)
                for bi in range(4):
                    i = t.matmul(out=pd[1][0:1, 512 + bi * 128:512 + (bi + 1) * 128],
                                 lhsT=fcol[:, bi * 4 + 2:bi * 4 + 3], rhs=identf[:],
                                 start=True, stop=True)
                S.inc(i, "pe")
            mk_qrowp = S.val("pe")

            @block.vector
            def _(v, mk_qrowp=mk_qrowp):
                S.wait(v, "pe", mk_qrowp)
                i = v.tensor_copy(out=qrow[:], in_=pd[1][0:1, 512:1024])
                S.inc(i, "dv")
            mk_qrow = S.val("dv")

            @block.tensor
            def _(t, mk_qrow=mk_qrow):
                S.wait(t, "dv", mk_qrow)
                i = t.matmul(out=pd[1][:, 0:512], lhsT=ones_row[:], rhs=qrow[:],
                             start=True, stop=True)
                S.inc(i, "pe")
            mk_qrep = S.val("pe")

            guq_marks = []

            @block.vector
            def _(v, mk_qrep=mk_qrep):
                S.wait(v, "pe", mk_qrep)
                for sl_id in range(32):
                    i = v.tensor_tensor(out=guqT[:, sl_id * GTOK:(sl_id + 1) * GTOK],
                                        in0=guT[:, sl_id * GTOK:(sl_id + 1) * GTOK],
                                        in1=pd[1][:, 0:512], op=ALU.mult)
                    S.inc(i, "dv")
                    guq_marks.append(S.val("dv") + sl_id)
            guq_marks = [S.val("dv") - 32 + 1 + k for k in range(32)]
            mk_guq = S.val("dv")

            if grp == 0 and "d_fc" in dbg_aps:
                @block.gpsimd
                def _(g, mk_fcol=mk_fcol):
                    S.wait(g, "dv", mk_fcol)
                    g.dma_start(out=dbg_aps["d_fc"], in_=fcol[:]).then_inc(S.h["qst"], 16)
                S.c["qst"] += 16
            if grp == 0 and "d_guq" in dbg_aps:
                @block.gpsimd
                def _(g, mk_guq=mk_guq):
                    S.wait(g, "dv", mk_guq)
                    g.dma_start(out=dbg_aps["d_guq"], in_=guqT[:, 0:GTOK]
                                ).then_inc(S.h["qst"], 16)
                S.c["qst"] += 16

            dq_done = []
            wd_marks = []
            for sg in range(2):
                for dc in range(32):
                    ppd = dc % 2

                    @block.gpsimd
                    def _(g, dc=dc, ppd=ppd):
                        if len(wd_marks) >= 2:
                            S.wait(g, "pe", wd_marks[-2])
                        S.wait(g, f"qd{ppd}", S.val(f"qd{ppd}"))
                        g.dma_start(out=wdslab[:, ppd * 1024:(ppd + 1) * 1024],
                                    in_=d_wdq[dc * 128:(dc + 1) * 128, :]
                                    ).then_inc(S.h[f"qd{ppd}"], 16)
                    S.c[f"qd{ppd}"] += 16
                    mk_wd = S.val(f"qd{ppd}")

                    @block.vector
                    def _(v, dc=dc, sg=sg, ppd=ppd, mk_g=guq_marks[dc]):
                        S.wait(v, "dv", mk_g)
                        if len(wd_marks) >= 2:
                            S.wait(v, "pe", wd_marks[-2])
                        i = v.tensor_copy(out=gcvt[:, ppd * 256:(ppd + 1) * 256],
                                          in_=guqT[:, dc * GTOK + sg * 256:
                                                   dc * GTOK + (sg + 1) * 256])
                        S.inc(i, "dv")
                    mk_cvt = S.val("dv")

                    @block.tensor
                    def _(t, dc=dc, sg=sg, ppd=ppd, mk_wd=mk_wd, mk_cvt=mk_cvt):
                        S.wait(t, f"qd{ppd}", mk_wd)
                        S.wait(t, "dv", mk_cvt)
                        for bi2 in range(2):
                            for half in range(2):
                                i = t.matmul(
                                    out=pd[sg * 2 + bi2][:, half * 512:(half + 1) * 512],
                                    lhsT=gcvt[:, ppd * 256 + bi2 * 128:
                                              ppd * 256 + (bi2 + 1) * 128],
                                    rhs=wdslab[:, ppd * 1024 + half * 512:
                                               ppd * 1024 + (half + 1) * 512],
                                    start=(dc == 0), stop=(dc == 31))
                        S.inc(i, "pe")
                    wd_marks.append(S.val("pe"))
                dq_done.append(S.val("pe"))

                for bi2 in range(2):
                    bi = sg * 2 + bi2
                    blk2 = grp * 4 + bi

                    @block.sync
                    def _(sy, blk2=blk2, bi2=bi2, m=xrr_marks[-1] if False else 0):
                        if xrr_marks:
                            S.wait(sy, "dv", xrr_marks[-1])
                        S.wait(sy, "qv", S.val("qv"))
                        sy.dma_start(out=xrr[:, bi2 * D:(bi2 + 1) * D],
                                     in_=ap_xr[blk2 * 128:(blk2 + 1) * 128, :]
                                     ).then_inc(S.h["qv"], 16)
                    S.c["qv"] += 16
                    mk_xrr = S.val("qv")

                    @block.vector
                    def _(v, bi=bi, bi2=bi2, m=dq_done[-1], mk_xrr=mk_xrr):
                        S.wait(v, "pe", m)
                        S.wait(v, "qv", mk_xrr)
                        i = v.scalar_tensor_tensor(
                            out=ffn_g[:, bi * D:(bi + 1) * D], in0=pd[bi][:],
                            scalar=fcol[:, bi * 4 + 3:bi * 4 + 4],
                            in1=xrr[:, bi2 * D:(bi2 + 1) * D], op0=ALU.mult, op1=ALU.add)
                        S.inc(i, "dv")
                    xrr_marks.append(S.val("dv"))
            mk_ffn = S.val("dv")
            if grp == 0 and "d_ffn" in dbg_aps:
                @block.gpsimd
                def _(g, mk_ffn=mk_ffn):
                    S.wait(g, "dv", mk_ffn)
                    g.dma_start(out=dbg_aps["d_ffn"], in_=ffn_g[:]).then_inc(S.h["qst"], 16)
                S.c["qst"] += 16

            if grp + 1 < NGRP:
                pending_xqT = emit_xq(grp + 1)

            for bi in range(4):
                blk = grp * 4 + bi

                @block.sync
                def _(sy, blk=blk, bi=bi, mk_ffn=mk_ffn):
                    if grp == 0 and bi == 0:
                        S.wait(sy, "qst", mk_ccin)
                    S.wait(sy, "dv", mk_ffn)
                    if st_marks:
                        S.wait(sy, "qst", st_marks[-1])
                    S.wait(sy, "qcc", S.val("qcc"))
                    S.wait(sy, "qcb", S.val("qcb"))
                    sy.dma_start(out=ccA[:].rearrange("p (b f) -> p b f", b=8),
                                 in_=d_ccout.ap().rearrange("(b p) t -> p b t", p=128)
                                 [:, :, blk * 128:(blk + 1) * 128]
                                 ).then_inc(S.h["qcc"], 16)
                    sy.dma_start(out=ccB[:].rearrange("p (b f) -> p b f", b=8),
                                 in_=d_ccout.ap().rearrange("(b p) t -> p b t", p=128)
                                 [:, :, TOWN + blk * 128:TOWN + (blk + 1) * 128]
                                 ).then_inc(S.h["qcb"], 16)
                S.c["qcc"] += 16
                S.c["qcb"] += 16
                mk_cc = S.val("qcc")
                mk_cb = S.val("qcb")

                @block.vector
                def _(v, mk_cc=mk_cc, mk_cb=mk_cb):
                    S.wait(v, "qcc", mk_cc)
                    S.wait(v, "qcb", mk_cb)
                    i = v.tensor_tensor(out=o2f[:], in0=ccB[:], in1=ccA[:],
                                        op=ALU.subtract)
                    S.guard(v, i, "dv")
                    i = v.scalar_tensor_tensor(out=osel[:], in0=o2f[:], scalar=hpcol[:],
                                               in1=ccA[:], op0=ALU.mult, op1=ALU.add)
                    S.guard(v, i, "dv")
                    i = v.tensor_tensor(out=o2f[:], in0=osel[:], in1=osel[:],
                                        op=ALU.mult)
                    S.inc(i, "dv")
                mk_osel = S.val("dv")
                if grp == 0 and bi == 0 and "d_cca" in dbg_aps:
                    @block.gpsimd
                    def _(g, mk_osel=mk_osel):
                        S.wait(g, "dv", mk_osel)
                        g.dma_start(out=dbg_aps["d_cca"], in_=ccA[:]).then_inc(S.h["qst"], 16)
                        g.dma_start(out=dbg_aps["d_osel"], in_=osel[:]).then_inc(S.h["qst"], 16)
                    S.c["qst"] += 32

                mk_o2 = mk_osel

                @block.tensor
                def _(t, mk_o2=mk_o2, mk_ffn=mk_ffn):
                    S.wait(t, "dv", mk_o2)
                    S.wait(t, "dv", mk_ffn)
                    for cht in range(8):
                        i = t.matmul(out=pd[0][0:1, 0:128],
                                     lhsT=ones_col[:],
                                     rhs=o2f[:, cht * 128:(cht + 1) * 128],
                                     start=(cht == 0), stop=(cht == 7))
                    S.inc(i, "pe")
                mk_ssq = S.val("pe")

                @block.vector
                def _(v, mk_ssq=mk_ssq):
                    S.wait(v, "pe", mk_ssq)
                    i = v.tensor_scalar(out=rrow[0:1, 0:128], in0=pd[0][0:1, 0:128],
                                        scalar1=1.0 / D, scalar2=1e-5,
                                        op0=ALU.mult, op1=ALU.add)
                    S.inc(i, "dv")
                mk_r1 = S.val("dv")

                @block.scalar
                def _(sc, mk_r1=mk_r1):
                    S.wait(sc, "dv", mk_r1)
                    i = sc.activation(out=rrow[0:1, 128:256], in_=rrow[0:1, 0:128],
                                      func=AF.Ln)
                    S.inc(i, "sc")
                    S.wait(sc, "sc", S.val("sc"))
                    i = sc.activation(out=rrow[0:1, 0:128], in_=rrow[0:1, 128:256],
                                      func=AF.Exp, scale=-0.5)
                    S.inc(i, "sc")
                mk_rs = S.val("sc")

                @block.tensor
                def _(t, mk_rs=mk_rs):
                    S.wait(t, "sc", mk_rs)
                    i = t.matmul(out=pd[0][:, 128:256],
                                 lhsT=ones_row[:],
                                 rhs=rrow[0:1, 0:128], start=True, stop=True)
                    S.inc(i, "pe")
                mk_rrep = S.val("pe")

                @block.vector
                def _(v, mk_rrep=mk_rrep):
                    S.wait(v, "pe", mk_rrep)
                    for cht in range(8):
                        i = v.tensor_tensor(out=rmsT[:, cht * 128:(cht + 1) * 128],
                                            in0=osel[:, cht * 128:(cht + 1) * 128],
                                            in1=pd[0][:, 128:256], op=ALU.mult)
                    S.inc(i, "dv")
                mk_rmsT = S.val("dv")

                @block.tensor
                def _(t, mk_rmsT=mk_rmsT, mk_woTg=mk_woTg):
                    S.wait(t, "dv", max(mk_rmsT, mk_woTg))
                    for half in range(2):
                        for cht in range(8):
                            i = t.matmul(out=pd[1][:, half * 512:(half + 1) * 512],
                                         lhsT=rmsT[:, cht * 128:(cht + 1) * 128],
                                         rhs=woTg[:, cht * 1024 + half * 512:
                                                  cht * 1024 + (half + 1) * 512],
                                         start=(cht == 0), stop=(cht == 7))
                    S.inc(i, "pe")
                mk_attn = S.val("pe")

                @block.vector
                def _(v, bi=bi, mk_attn=mk_attn):
                    S.wait(v, "pe", mk_attn)
                    i = v.tensor_tensor(out=outbuf[:], in0=pd[1][:],
                                        in1=ffn_g[:, bi * D:(bi + 1) * D], op=ALU.add)
                    S.inc(i, "dv")
                mk_out = S.val("dv")

                @block.gpsimd
                def _(g, blk=blk, mk_out=mk_out):
                    S.wait(g, "dv", mk_out)
                    S.wait(g, "qst", S.val("qst"))
                    g.dma_start(out=ap_out[blk * 128:(blk + 1) * 128, :],
                                in_=outbuf[:]).then_inc(S.h["qst"], 16)
                S.c["qst"] += 16
                st_marks.append(S.val("qst"))

        @block.gpsimd
        def _(g):
            S.wait(g, "qst", S.val("qst"))
            g.engine_nop()

    return nc


_NC_CACHE = {}


def _get_nc():
    if "nc" not in _NC_CACHE:
        _NC_CACHE["nc"] = build_nc()
    return _NC_CACHE["nc"]


def kernel(L0_attn_hidden_states, L0_attn_Wq, L0_attn_Wk, L0_attn_Wv, L0_attn_Wf,
           L0_attn_g_w, L0_attn_Wo, L0_ffn_w_gate, L0_ffn_w_up, L0_ffn_w_down):
    x = np.ascontiguousarray(np.asarray(L0_attn_hidden_states, dtype=np.float32))
    Wq = np.asarray(L0_attn_Wq, dtype=np.float32)
    Wk = np.asarray(L0_attn_Wk, dtype=np.float32)
    Wv = np.asarray(L0_attn_Wv, dtype=np.float32)
    Wf = np.asarray(L0_attn_Wf, dtype=np.float32)
    gw = np.ascontiguousarray(np.asarray(L0_attn_g_w, dtype=np.float32).reshape(8, 128).T)
    woT = np.ascontiguousarray(np.asarray(L0_attn_Wo, dtype=np.float32).T)
    wgT = np.ascontiguousarray(np.asarray(L0_ffn_w_gate, dtype=np.float32).T)
    wuT = np.ascontiguousarray(np.asarray(L0_ffn_w_up, dtype=np.float32).T)
    wdT = np.ascontiguousarray(np.asarray(L0_ffn_w_down, dtype=np.float32).T)

    wqT = np.ascontiguousarray(Wq.T)
    wkT = np.ascontiguousarray(Wk.T)
    wvT = np.ascontiguousarray(Wv.T)
    wfT = np.ascontiguousarray(Wf.T)

    in_maps = []
    for c in range(8):
        b, hp = c // 2, c % 2
        in_maps.append({
            "x": x[b],
            "xr": np.ascontiguousarray(x[b, hp * TOWN:(hp + 1) * TOWN]),
            "wqT": wqT, "wkT": wkT, "wvT": wvT, "wfT": wfT,
            "woT": woT, "gw": gw,
            "wgT": wgT, "wuT": wuT, "wdT": wdT,
            "hpc": np.full((128, 1), float(hp), np.float32),
        })

    nc = _get_nc()
    res = run_bass_kernel_spmd(nc, in_maps, core_ids=list(range(8)))
    out = np.empty((4, T, D), np.float32)
    for c in range(8):
        b, hp = c // 2, c % 2
        out[b, hp * TOWN:(hp + 1) * TOWN] = res.results[c]["out"]
    return out
